# version 23
# baseline (speedup 1.0000x reference)
import numpy as np
import ml_dtypes
import concourse.bass as bass
import concourse.mybir as mybir
from concourse.bass_utils import run_bass_kernel_spmd
from contextlib import ExitStack

F32 = mybir.dt.float32
BF16 = mybir.dt.bfloat16
I32 = mybir.dt.int32
AF = mybir.ActivationFunctionType
ALU = mybir.AluOpType
AX = mybir.AxisListType.X

P = 128
T = 1024
NT = 8
H = 16
DM = 2048
NI = DM // P
DFF = 5504
NF = DFF // P
EPS_ATT = 1e-6
EPS_LN = 1e-5
EPS_FFN = 1e-5
RNK = 8
K17 = float(2.0 ** 17)
K17I = float(2.0 ** -17)

DEBUG = False

WD_RANGES = [(NF * c // RNK, NF * (c + 1) // RNK) for c in range(RNK)]
WD_PAD = 6
GU_ROWS = 2 * DFF
OS_W = [512] * 10 + [384]
OS_OFF = [sum(OS_W[:i]) for i in range(len(OS_W))]


class Op:
    __slots__ = ("eng", "fn", "is_dma", "idx", "deps", "tagged", "tag_val",
                 "waits", "stream", "fence_wait", "group")
    def __init__(self, eng, fn, is_dma, idx, stream):
        self.eng, self.fn, self.is_dma, self.idx = eng, fn, is_dma, idx
        self.deps = set(); self.tagged = False; self.tag_val = None
        self.waits = {}; self.stream = stream; self.fence_wait = None
        self.group = None

class Builder:
    def __init__(self):
        self.ops = []
        self.res = {}

    def _st(self, r):
        if r not in self.res:
            self.res[r] = {"w": None, "r": set()}
        return self.res[r]

    def op(self, eng, fn, reads=(), writes=(), acquire=(), commit=(),
           is_dma=False, stream=None, group=None):
        idx = len(self.ops)
        stream = stream or (eng, "d" if is_dma else "c")
        o = Op(eng, fn, is_dma, idx, stream)
        o.group = group
        self.ops.append(o)
        for r in reads:
            st = self._st(r)
            if st["w"] is not None:
                o.deps.add(st["w"])
            st["r"].add(idx)
        for r in writes:
            st = self._st(r)
            if st["w"] is not None:
                o.deps.add(st["w"])
            o.deps.update(st["r"])
            st["r"] = set(); st["w"] = idx
        for r in acquire:
            st = self._st(r)
            if st["w"] is not None:
                o.deps.add(st["w"])
            o.deps.update(st["r"])
            st["r"] = set(); st["w"] = None
        for r in commit:
            st = self._st(r)
            st["w"] = idx; st["r"] = set()
        o.deps.discard(idx)
        return idx

    def finalize(self, nc, ctx):
        ops = self.ops
        heads = {}
        for o in ops:
            if o.group is None:
                continue
            key = (o.group, o.stream)
            if key not in heads:
                heads[key] = o
            else:
                h = heads[key]
                kept = set()
                for d in o.deps:
                    if d < h.idx:
                        h.deps.add(d)
                    elif d != h.idx:
                        kept.add(d)
                o.deps = kept
        by_eng = {}
        for o in ops:
            by_eng.setdefault(o.eng, []).append(o)

        def compute_bursts():
            burst_of = {}
            burst_members = {}
            nb = {}
            for eng, olist in by_eng.items():
                open_bursts = {}
                for o in olist:
                    if o.deps:
                        open_bursts.clear()
                    if o.is_dma:
                        s = o.stream
                        if s not in open_bursts:
                            nb[s] = nb.get(s, -1) + 1
                            open_bursts[s] = nb[s]
                        bid = (s, open_bursts[s])
                        burst_of[o.idx] = bid
                        burst_members.setdefault(bid, []).append(o.idx)
            return burst_of, burst_members

        for _round in range(40):
            burst_of, burst_members = compute_bursts()
            changed = False
            for o in ops:
                for d in list(o.deps):
                    do = ops[d]
                    if not do.is_dma:
                        continue
                    mem = burst_members[burst_of[d]]
                    if mem[-1] > o.idx:
                        k = next(i for i in mem if i > o.idx)
                        prev = mem[mem.index(k) - 1]
                        if prev not in ops[k].deps:
                            ops[k].deps.add(prev)
                            changed = True
            if not changed:
                break
        else:
            raise RuntimeError("fence injection did not converge")

        for o in ops:
            if o.is_dma:
                o.tagged = True
            for d in o.deps:
                ops[d].tagged = True
        tagcount = {}
        for o in ops:
            k = o.stream
            c = tagcount.get(k, 0)
            if o.tagged:
                c += 1
                o.tag_val = c * (16 if o.is_dma else 1)
            tagcount[k] = c
        burst_end_val = {}
        for bid, mem in burst_members.items():
            burst_end_val[bid] = max(ops[i].tag_val for i in mem)
        for o in ops:
            w = {}
            for d in o.deps:
                do = ops[d]
                if do.is_dma:
                    v = burst_end_val[burst_of[d]]
                else:
                    v = do.tag_val
                w[do.stream] = max(w.get(do.stream, 0), v)
            o.waits = w
        for bid, mem in burst_members.items():
            first = ops[mem[0]]
            s = first.stream
            prior = first.tag_val - 16
            if prior > 0:
                first.waits[s] = max(first.waits.get(s, 0), prior)
                first.fence_wait = (s, prior)
        self.sems = {}
        for k in sorted({o.stream for o in self.ops}, key=str):
            self.sems[k] = ctx.enter_context(nc.semaphore(f"s_{k[0]}_{k[1]}"))
        return tagcount

    def replay(self, block):
        by_eng = {}
        for o in self.ops:
            by_eng.setdefault(o.eng, []).append(o)
        hw = {}
        def mk(eng_name, ops):
            def body(eng):
                for o in ops:
                    wl = sorted(o.waits.items(), key=lambda kv: str(kv[0]))
                    for k, v in wl:
                        hk = (eng_name, k)
                        forced = o.fence_wait is not None and o.fence_wait[0] == k
                        if not forced and hw.get(hk, -1) >= v:
                            continue
                        eng.wait_ge(self.sems[k], v)
                        hw[hk] = max(hw.get(hk, -1), v)
                    inst = o.fn(eng)
                    if o.tagged:
                        inst.then_inc(self.sems[o.stream], 16 if o.is_dma else 1)
            return body
        for eng_name, ops in by_eng.items():
            getattr(block, eng_name)(mk(eng_name, ops))


def build_nc():
    nc = bass.Bass("TRN2", target_bir_lowering=False)
    B = Builder()
    ctx = ExitStack()
    E = ctx.enter_context
    OPS = B.op

    ap_in = lambda n, s: nc.dram_tensor(n, s, F32, kind="ExternalInput").ap()
    x_sl = ap_in("x_sl", [T, DM])
    wq_d = ap_in("wq", [DM, DM]); wk_d = ap_in("wk", [DM, DM])
    wv_d = ap_in("wv", [DM, DM]); wo_d = ap_in("wo", [DM, DM])
    bq_d = ap_in("bq", [DM]); bk_d = ap_in("bk", [DM])
    bv_d = ap_in("bv", [DM]); bo_d = ap_in("bo", [DM])
    wg_sh = ap_in("wg_sh", [DFF, 256])
    wu_sh = ap_in("wu_sh", [DFF, 256])
    wd_sh = ap_in("wd_sh", [DM, WD_PAD * P])
    pm_d = ap_in("pm", [P, RNK])
    out_sl = nc.dram_tensor("out_sl", [T, DM], F32, kind="ExternalOutput").ap()

    xbf_dr = nc.dram_tensor("xbf_dr", [T, DM], BF16, kind="Internal").ap()
    xq_dr = nc.dram_tensor("xq_dr", [T, DM], BF16, kind="Internal").ap()
    hid_dr = nc.dram_tensor("hid_dr", [T, DFF], F32, kind="Internal").ap()
    hq_dr = nc.dram_tensor("hq_dr", [T, DFF], BF16, kind="Internal").ap()
    wag_g_in = nc.dram_tensor("wag_g_in", [GU_ROWS, 256], BF16, kind="Internal").ap()
    wag_g_out = nc.dram_tensor("wag_g_out", [RNK * GU_ROWS, 256], BF16, kind="Internal").ap()
    wag_d_in = nc.dram_tensor("wag_d_in", [DM, WD_PAD * P], BF16, kind="Internal").ap()
    wag_d_out = nc.dram_tensor("wag_d_out", [RNK * DM, WD_PAD * P], BF16, kind="Internal").ap()
    kag_in = nc.dram_tensor("kag_in", [P, DM + H], F32, kind="Internal").ap()
    kag_out = nc.dram_tensor("kag_out", [RNK * P, DM + H], F32, kind="Internal").ap()

    dbg = {}
    if DEBUG:
        for nm, shape in (("d_qf", [DM, T]), ("d_kf", [DM, T]), ("d_v", [T, DM]),
                          ("d_kvtot", [P, DM + H]), ("d_att", [DM, T]),
                          ("d_x1n", [T, DM]), ("d_xq", [T, DM]),
                          ("d_wgq", [GU_ROWS, 256]), ("d_hid", [T, DFF]),
                          ("d_hq", [T, DFF])):
            dbg[nm] = nc.dram_tensor(nm, shape, F32, kind="ExternalOutput").ap()

    regA = E(nc.sbuf_tensor("regA", [P, 16 * 1024], BF16))
    regB = E(nc.sbuf_tensor("regB", [P, 16 * 1024], BF16))
    regC = E(nc.sbuf_tensor("regC", [P, 32 * 1024], BF16))
    regD = E(nc.sbuf_tensor("regD", [P, 16 * 1024], BF16))
    regX = E(nc.sbuf_tensor("regX", [P, 3712], F32))
    Skv = E(nc.sbuf_tensor("Skv", [P, DM], F32))
    Sks = E(nc.sbuf_tensor("Sks", [P, H], F32))
    Skv_bf = E(nc.sbuf_tensor("Skv_bf", [P, DM], BF16))
    Sks_bf = E(nc.sbuf_tensor("Sks_bf", [P, H], BF16))
    brow = E(nc.sbuf_tensor("brow", [1, DM], BF16))
    mask_f = E(nc.sbuf_tensor("mask_f", [P, P], F32))
    ident_bf = E(nc.sbuf_tensor("ident_bf", [P, P], BF16))
    ones_col = E(nc.sbuf_tensor("ones_col", [P, 1], BF16))
    ones_row = E(nc.sbuf_tensor("ones_row", [1, P], BF16))
    t1 = E(nc.sbuf_tensor("t1", [P, 512], F32))
    t2 = E(nc.sbuf_tensor("t2", [P, 512], F32))
    t3 = E(nc.sbuf_tensor("t3", [P, 512], F32))
    wstage = E(nc.sbuf_tensor("wstage", [P, 768], F32))
    wbuf_t = E(nc.sbuf_tensor("wbuf", [P, 4, 512], BF16))
    wbuf = [wbuf_t[:, i, :] for i in range(4)]
    wdbuf = None
    Asb_t = E(nc.sbuf_tensor("Asb", [P, 2, P], BF16))
    Asb = [Asb_t[:, 0, :], Asb_t[:, 1, :]]
    attm_t = E(nc.sbuf_tensor("attm", [P, 2, P], BF16))
    attm = [attm_t[:, 0, :], attm_t[:, 1, :]]
    scl = E(nc.sbuf_tensor("scl", [P, P], F32))
    hamax = E(nc.sbuf_tensor("hamax", [P, NT], F32))
    pm_sb = E(nc.sbuf_tensor("pm_sb", [P, RNK], F32))
    bnst = E(nc.sbuf_tensor("bnst", [P, 4, 6], F32))
    PS = E(nc.psum_tensor("PS", [P, 4096], F32))

    BQ, BK, NBQ, NBK = 0, 16, 32, 48
    ZEPS, ZR = 64, 80
    QSC, QRI = 96, 104
    AMX, SFC, ISF = 111, 112, 113
    BNA, LNV, LNR, LNB = 114, 116, 117, 118
    AMXT, HAM2, SF2, IS2 = 119, 120, 121, 122
    sc = lambda c, n=1: scl[:, c:c + n]

    x_fm = [regA[:, ic * 1024:(ic + 1) * 1024] for ic in range(NI)]
    att_fm = x_fm
    rA = lambda ic: ("rA", ic)
    qf_fm = [regB[:, ic * 1024:(ic + 1) * 1024] for ic in range(NI)]
    xq_fm = qf_fm
    rB = lambda ic: ("rB", ic)
    kf_fm = [regC[:, ic * 1024:(ic + 1) * 1024] for ic in range(NI)]
    kf_tm = [regC[:, 16 * 1024 + tt * 2048:16 * 1024 + (tt + 1) * 2048]
             for tt in range(NT)]
    rCf = lambda ic: ("rC", ic)
    rCt = lambda tt: ("rC", 16 + tt)
    v_tm = [regD[:, tt * 2048:(tt + 1) * 2048] for tt in range(NT)]
    rD = lambda tt: ("rD", tt)
    regC_f32 = regC.bitcast(F32)
    x1n = [regC_f32[:, tt * 2048:(tt + 1) * 2048] for tt in range(NT)]
    rX1 = lambda tt, q: ("rC", "x1", tt, q)
    def rC_x1(tt, q):
        bf_block = (tt * 4096 + q * 1024) // 1024
        if bf_block < 16:
            return ("rC", bf_block)
        return ("rC", 16 + (bf_block - 16) // 2)
    wgbuf = lambda par, ic: regA[:, (par * NI + ic) * 512:(par * NI + ic + 1) * 512]
    rWg = lambda par, ic: ("rA", (par * NI + ic) // 2)
    wubuf = lambda par, ic: regD[:, (par * NI + ic) * 512:(par * NI + ic + 1) * 512]
    rWu = lambda par, ic: ("rD", (par * NI + ic) // 4)
    def hq_fm(fc):
        if fc < 32:
            return regA[:, fc * 512:(fc + 1) * 512]
        return regD[:, (fc - 32) * 512:(fc - 31) * 512]
    def rHq(fc):
        if fc < 32:
            return ("rA", fc // 2)
        return ("rD", (fc - 32) // 4)
    xq_tm = [regD[:, tt * 2048:(tt + 1) * 2048] for tt in range(NT)]
    regB_f32 = regB.bitcast(F32)
    out_tile = [regB_f32[:, 0:2048], regB_f32[:, 2048:4096]]
    wdbuf = [regB[:, 8 * 1024 + i * 512:8 * 1024 + (i + 1) * 512]
             for i in range(4)]
    rOut = lambda par: [("rB", 4 * par), ("rB", 4 * par + 1),
                        ("rB", 4 * par + 2), ("rB", 4 * par + 3)]
    regX_i32 = regX.bitcast(I32)
    regX_bf = regX.bitcast(BF16)
    rXb = lambda c0, c1: [("rX", b) for b in range(c0 // 512, (c1 + 511) // 512)]
    xc_tmp = [regB[:, 12 * 1024:12 * 1024 + 2048],
              regB[:, 14 * 1024:14 * 1024 + 2048]]
    rXc = [[("rB", 12), ("rB", 13)], [("rB", 14), ("rB", 15)]]

    bank = lambda b, w=512: PS[:, b * 512:b * 512 + w]
    PS_bf = PS.bitcast(BF16)
    psq = lambda b, q: ("ps", b, q)
    psB = lambda b, w=512: [("ps", b, q) for q in range((w + 127) // 128)]

    OPS("gpsimd", lambda e: e.memset(mask_f[:], 1.0), writes=[("mask",)])
    OPS("gpsimd", lambda e: e.affine_select(
        out=mask_f[:], in_=mask_f[:], compare_op=ALU.is_ge, fill=0.0,
        base=0, pattern=[[1, P]], channel_multiplier=-1),
        reads=[("mask",)], writes=[("mask",)])
    OPS("gpsimd", lambda e: e.memset(ident_bf[:], 1.0), writes=[("ident",)])
    OPS("gpsimd", lambda e: e.affine_select(
        out=ident_bf[:], in_=ident_bf[:], compare_op=ALU.is_equal, fill=0.0,
        base=0, pattern=[[1, P]], channel_multiplier=-1),
        reads=[("ident",)], writes=[("ident",)])
    OPS("vector", lambda e: e.memset(ones_col[:], 1.0), writes=[("ones_c",)])
    OPS("vector", lambda e: e.memset(ones_row[:], 1.0), writes=[("ones_r",)])
    OPS("vector", lambda e: e.memset(Skv[:], 0.0), writes=[("Skv",)])
    OPS("vector", lambda e: e.memset(Sks[:], 0.0), writes=[("Sks",)])
    OPS("vector", lambda e: e.memset(hamax[:], 0.0),
        writes=[("ham", tt) for tt in range(NT)])
    OPS("vector", lambda e: e.memset(scl[:, 123:124], EPS_ATT),
        writes=[("c_eps",)])
    OPS("sync", lambda e: e.dma_start(
        out=sc(BQ, NI), in_=bq_d.rearrange("(o p) -> p o", p=P),
        allow_slow_non_contiguous=True),
        writes=[("c_bq",)], is_dma=True)
    OPS("sync", lambda e: e.dma_start(
        out=sc(BK, NI), in_=bk_d.rearrange("(o p) -> p o", p=P),
        allow_slow_non_contiguous=True),
        writes=[("c_bk",)], is_dma=True)
    OPS("sync", lambda e: e.dma_start(out=pm_sb[:], in_=pm_d),
        writes=[("pm",)], is_dma=True)
    OPS("gpsimd", lambda e: e.dma_start(
        out=brow[:], in_=bv_d.unsqueeze(0)),
        writes=[("brow",)], is_dma=True)
    OPS("vector", lambda e: e.tensor_scalar(
        sc(NBQ, NI), sc(BQ, NI), -1.0, None, ALU.mult),
        reads=[("c_bq",)], writes=[("c_nbq",)])
    OPS("vector", lambda e: e.tensor_scalar(
        sc(NBK, NI), sc(BK, NI), -1.0, None, ALU.mult),
        reads=[("c_bk",)], writes=[("c_nbk",)])

    for tt in range(NT):
        buf = xc_tmp[tt % 2]
        rr = rXc[tt % 2]
        OPS("gpsimd", lambda e, tt=tt, buf=buf: e.dma_start(
            out=buf, in_=x_sl[tt * P:(tt + 1) * P, :]),
            writes=rr, is_dma=True)
        OPS("gpsimd", lambda e, tt=tt, buf=buf: e.dma_start(
            out=xbf_dr[tt * P:(tt + 1) * P, :], in_=buf),
            reads=rr, writes=[("xbf", tt)], is_dma=True)
    for ic in range(NI):
        OPS("sync", lambda e, ic=ic: e.dma_start_transpose(
            x_fm[ic], xbf_dr[:, ic * P:(ic + 1) * P]),
            reads=[("xbf", tt2) for tt2 in range(NT)], writes=[rA(ic)],
            is_dma=True, group=("xfm",))

    regB_f32q = regB.bitcast(F32)
    _qpar = [0]

    def quant_tile(src_ap, dst_ap, G):
        W = G * P
        pp = _qpar[0] & 1
        _qpar[0] += 1
        if pp == 0:
            stage = wstage[:, 0:W]
            rstage = [("qstage", 0)]
            qbf = regX_bf[:, 6400:6400 + W]
            rqbf = rXb(3200, 3200 + (W + 1) // 2)
        else:
            stage = regB_f32q[:, 0:W]
            rstage = [("qstage", 1), ("rB", 0), ("rB", 1)]
            qbf = regB[:, 1536:1536 + W]
            rqbf = [("rB", 1), ("rB", 2)]
        OPS("scalar", lambda e: e.dma_start(out=stage, in_=src_ap),
            writes=rstage, is_dma=True)
        aw = regX[:, 0:W]
        OPS("scalar", lambda e: e.activation(aw, stage, AF.Abs),
            reads=rstage, writes=rXb(0, W))
        hi_i = regX_i32[:, 768:768 + W]
        OPS("scalar", lambda e: e.activation(hi_i, aw, AF.Identity, scale=K17),
            reads=rXb(0, W), writes=rXb(768, 768 + W))
        lo = regX[:, 1536:1536 + W]
        OPS("vector", lambda e: e.scalar_tensor_tensor(
            lo, hi_i, -K17I, aw, ALU.mult, ALU.add),
            reads=rXb(768, 768 + W) + rXb(0, W), writes=rXb(1536, 1536 + W))
        rh = regX[:, 2304:2304 + G]
        rl = regX[:, 2320:2320 + G]
        OPS("vector", lambda e: e.tensor_reduce(
            rh, hi_i.rearrange("p (g e) -> p g e", g=G), AX, ALU.add),
            reads=rXb(768, 768 + W), writes=rXb(2304, 2310))
        OPS("vector", lambda e: e.tensor_reduce(
            rl, lo.rearrange("p (g e) -> p g e", g=G), AX, ALU.add),
            reads=rXb(1536, 1536 + W), writes=rXb(2320, 2326))
        OPS("vector", lambda e: e.scalar_tensor_tensor(
            sc(QSC, G), rh, K17I, rl, ALU.mult, ALU.add),
            reads=rXb(2304, 2326), writes=[("c_qsc",)])
        OPS("vector", lambda e: e.tensor_scalar(
            sc(QSC, G), sc(QSC, G), 1.0 / P, EPS_FFN, ALU.mult, ALU.add),
            reads=[("c_qsc",)], writes=[("c_qsc",)])
        OPS("vector", lambda e: e.reciprocal(sc(QRI, G), sc(QSC, G)),
            reads=[("c_qsc",)], writes=[("c_qri",)])
        sgn = regX[:, 1536:1536 + W]
        OPS("scalar", lambda e: e.activation(sgn, stage, AF.Sign),
            reads=rstage + rXb(1536, 1536 + W),
            writes=rXb(1536, 1536 + W) + [("rX_sgn",)])
        m01 = regX[:, 2432:2432 + W]
        for g in range(G):
            OPS("vector", lambda e, g=g: e.tensor_scalar(
                m01[:, g * P:(g + 1) * P], aw[:, g * P:(g + 1) * P],
                sc(QRI + g), 0.5, ALU.mult, ALU.is_gt),
                reads=rXb(0, W) + [("c_qri",)],
                writes=[("rX_m", g)] + (rXb(2432, 2432 + W) if g == 0 else []))
        for g in range(G):
            OPS("vector", lambda e, g=g: e.scalar_tensor_tensor(
                qbf[:, g * P:(g + 1) * P], m01[:, g * P:(g + 1) * P],
                sc(QSC + g), sgn[:, g * P:(g + 1) * P],
                ALU.mult, ALU.mult),
                reads=[("rX_m", g), ("rX_sgn",), ("c_qsc",)] +
                      rXb(1536, 1536 + W) + rXb(2432, 2432 + W),
                writes=[("rX_qbf", pp, g)] + (rqbf if g == 0 else []))
        OPS("sync", lambda e: e.dma_start(out=dst_ap, in_=qbf),
            reads=[("rX_qbf", pp, g) for g in range(G)] + rqbf,
            writes=[("wag_w",)], is_dma=True)

    quant_jobs = []
    for r in range(NF):
        quant_jobs.append((wg_sh[r * P:(r + 1) * P, :],
                           wag_g_in[r * P:(r + 1) * P, :], 2))
    for r in range(NF):
        quant_jobs.append((wu_sh[r * P:(r + 1) * P, :],
                           wag_g_in[DFF + r * P:DFF + (r + 1) * P, :], 2))
    for r in range(NI):
        quant_jobs.append((wd_sh[r * P:(r + 1) * P, :],
                           wag_d_in[r * P:(r + 1) * P, :], WD_PAD))
    quant_jobs.reverse()

    def emit_quant(n):
        for _ in range(n):
            if quant_jobs:
                quant_tile(*quant_jobs.pop())

    def emit_ag1():
        emit_quant(len(quant_jobs))
        if DEBUG:
            OPS("gpsimd", lambda e: e.dma_start(out=dbg["d_wgq"], in_=wag_g_in),
                reads=[("wag_w",)], writes=[("dbgwg",)], is_dma=True)
        OPS("gpsimd", lambda e: e.collective_compute(
            "AllGather", ALU.bypass, replica_groups=[list(range(RNK))],
            ins=[wag_g_in], outs=[wag_g_out]),
            reads=[("wag_w",), ("qf", 15, 1)], writes=[("wag_out",)],
            stream=("cc", "1a"))
        OPS("gpsimd", lambda e: e.collective_compute(
            "AllGather", ALU.bypass, replica_groups=[list(range(RNK))],
            ins=[wag_d_in], outs=[wag_d_out]),
            reads=[("wag_w",), ("qf", 15, 1)], writes=[("wag_outd",)],
            stream=("cc", "1b"))

    def load_wslice(wdram, ic, oq, par, group=None):
        OPS("gpsimd", lambda e, ic=ic, oq=oq, par=par, wdram=wdram: e.dma_start(
            out=wbuf[par], in_=wdram[ic * P:(ic + 1) * P,
                                     oq * 512:(oq + 1) * 512]),
            writes=[("wbuf", par)], is_dma=True, group=group)

    for oq in range(4):
        for blk in range(0, NI, 4):
            for ic in range(blk, blk + 4):
                load_wslice(wv_d, ic, oq, ic % 4, group=("vw", oq, blk))
            for ic in range(blk, blk + 4):
                par = ic % 4
                first = ic == 0
                for tt in range(NT):
                    OPS("tensor", lambda e, ic=ic, tt=tt, par=par: e.matmul(
                        bank(tt), x_fm[ic][:, tt * P:(tt + 1) * P], wbuf[par],
                        start=(ic == 0), stop=False),
                        reads=[rA(ic), ("wbuf", par)],
                        acquire=psB(tt) if (first and tt >= 0) else ())
        for tt in range(NT):
            OPS("tensor", lambda e, oq=oq, tt=tt: e.matmul(
                bank(tt), ones_row[:], brow[:, oq * 512:(oq + 1) * 512],
                start=False, stop=True),
                reads=[("ones_r",), ("brow",)], commit=psB(tt))
        for tt in range(NT):
            OPS("scalar", lambda e, oq=oq, tt=tt: e.copy(
                v_tm[tt][:, oq * 512:(oq + 1) * 512], bank(tt)),
                reads=psB(tt), writes=[rD(tt), ("vtm", tt, oq)])
        emit_quant(15)

    def proj_fm(wdram, pb, nb, pres, nres, dst_fm, dblk, dres, qn=0,
                hook=None):
        for rnd in range(4):
            for blk in range(0, NI, 4):
                for ic in range(blk, blk + 4):
                    load_wslice(wdram, ic, rnd, ic % 4, group=(id(wdram), rnd, blk))
                for ic in range(blk, blk + 4):
                    par = ic % 4
                    for j in range(4):
                        for half in range(2):
                            b = 2 * j + half
                            first = ic == 0; last = ic == NI - 1
                            OPS("tensor",
                                lambda e, ic=ic, j=j, half=half, par=par, b=b: e.matmul(
                                    bank(b), wbuf[par][:, j * P:(j + 1) * P],
                                    x_fm[ic][:, half * 512:(half + 1) * 512],
                                    start=(ic == 0), stop=(ic == NI - 1)),
                                reads=[rA(ic), ("wbuf", par)],
                                acquire=psB(b) if first else (),
                                commit=psB(b) if last else ())
            if rnd == 0 and hook is not None:
                hook()
            for j in range(4):
                oc = rnd * 4 + j
                for half in range(2):
                    b = 2 * j + half
                    OPS("scalar", lambda e, b=b, oc=oc: e.activation(
                        t1[:], bank(b), AF.Relu, bias=sc(nb + oc), scale=-1.0),
                        reads=psB(b) + [nres], writes=[("t1",)])
                    OPS("scalar", lambda e: e.activation(
                        t2[:], t1[:], AF.Exp, scale=-1.0),
                        reads=[("t1",)], writes=[("t2",)])
                    OPS("scalar", lambda e, b=b, oc=oc: e.activation(
                        t3[:], bank(b), AF.Relu, bias=sc(pb + oc), scale=1.0),
                        reads=psB(b) + [pres], writes=[("t3",)])
                    OPS("vector",
                        lambda e, oc=oc, half=half: e.tensor_add(
                            dst_fm[oc][:, half * 512:(half + 1) * 512],
                            t2[:], t3[:]),
                        reads=[("t2",), ("t3",)],
                        writes=[dblk(oc), (dres, oc, half)])
            emit_quant(qn[rnd] if isinstance(qn, (list, tuple)) else qn)

    proj_fm(wk_d, BK, NBK, ("c_bk",), ("c_nbk",), kf_fm, rCf, "kf", qn=[14, 14, 9, 7])

    for oc in range(NI):
        for tt in range(NT):
            s8 = (oc * NT + tt) % 8
            pcol = s8 * 1024
            pres = psq(s8, 0)
            OPS("tensor", lambda e, oc=oc, tt=tt, pcol=pcol: e.transpose(
                PS_bf[:, pcol:pcol + P],
                kf_fm[oc][:, tt * P:(tt + 1) * P], ident_bf[:]),
                reads=[("kf", oc, tt // 4), ("ident",)],
                writes=[pres])
            OPS("scalar", lambda e, oc=oc, tt=tt, pcol=pcol: e.copy(
                kf_tm[tt][:, oc * P:(oc + 1) * P],
                PS_bf[:, pcol:pcol + P]),
                reads=[pres], writes=[rCt(tt), ("kt", tt, oc)])

    for h in range(H):
        b = h // 4
        col = (h % 4) * P
        for n in range(NT):
            first = n == 0 and h % 4 == 0
            last = n == NT - 1 and h % 4 == 3
            OPS("tensor", lambda e, h=h, n=n, b=b, col=col: e.matmul(
                PS[:, b * 512 + col:b * 512 + col + P],
                kf_tm[n][:, h * P:(h + 1) * P], v_tm[n][:, h * P:(h + 1) * P],
                start=(n == 0), stop=(n == NT - 1)),
                reads=[("kt", n, h), ("vtm", n, h // 4), rCt(n), rD(n)],
                acquire=[psq(b, h % 4)] if n == 0 else (),
                commit=[psq(b, h % 4)] if n == NT - 1 else ())
    for h in range(H):
        for n in range(NT):
            first = n == 0 and h == 0
            last = n == NT - 1 and h == H - 1
            OPS("tensor", lambda e, h=h, n=n: e.matmul(
                PS[:, 4 * 512 + h:4 * 512 + h + 1],
                kf_tm[n][:, h * P:(h + 1) * P], ones_col[:],
                start=(n == 0), stop=(n == NT - 1)),
                reads=[("kt", n, h), ("ones_c",), rCt(n)],
                acquire=[psq(4, 0)] if first else (),
                commit=[psq(4, 0)] if last else ())
    for b4 in range(4):
        tb = t1 if b4 % 2 == 0 else t2
        rt = ("t1",) if b4 % 2 == 0 else ("t2",)
        OPS("vector", lambda e, b4=b4, tb=tb: e.tensor_copy(tb[:], bank(b4)),
            reads=psB(b4), writes=[rt])
        OPS("sync", lambda e, b4=b4, tb=tb: e.dma_start(
            out=kag_in[:, b4 * 512:(b4 + 1) * 512], in_=tb[:]),
            reads=[rt], writes=[("kag_w",)], is_dma=True)
    OPS("vector", lambda e: e.tensor_copy(t3[:, 0:H], PS[:, 4 * 512:4 * 512 + H]),
        reads=[psq(4, 0)], writes=[("t3",)])
    OPS("sync", lambda e: e.dma_start(out=kag_in[:, DM:DM + H], in_=t3[:, 0:H]),
        reads=[("t3",)], writes=[("kag_w",)], is_dma=True)
    if DEBUG:
        OPS("sync", lambda e: e.dma_start(out=dbg["d_kvtot"], in_=kag_in),
            reads=[("kag_w",)], writes=[("dbg_kv",)], is_dma=True)
    def ag2_prefix_hook():
        OPS("gpsimd", lambda e: e.collective_compute(
            "AllGather", ALU.bypass, replica_groups=[list(range(RNK))],
            ins=[kag_in], outs=[kag_out]),
            reads=[("kag_w",)], writes=[("kag_out",)], stream=("cc", "2"))
        for r in range(RNK):
            if r % 2 == 0:
                st_kv = regX[:, 0:2048]
                r_kv = rXb(0, 2048)
                st_ks = regX[:, 2048:2048 + H]
                r_ks = rXb(2048, 2048 + H)
            else:
                st_kv = regB_f32q[:, 6144:8192]
                r_kv = [("rB", bb) for bb in range(12, 16)]
                st_ks = regX[:, 2064:2064 + H]
                r_ks = rXb(2064, 2064 + H)
            OPS("scalar", lambda e, r=r, st_kv=st_kv: e.dma_start(
                out=st_kv, in_=kag_out[r * P:(r + 1) * P, 0:DM]),
                reads=[("kag_out",)], writes=r_kv, is_dma=True)
            OPS("scalar", lambda e, r=r, st_ks=st_ks: e.dma_start(
                out=st_ks, in_=kag_out[r * P:(r + 1) * P, DM:DM + H]),
                reads=[("kag_out",)], writes=r_ks, is_dma=True)
            OPS("vector", lambda e, r=r, st_kv=st_kv: e.scalar_tensor_tensor(
                Skv[:], st_kv, pm_sb[:, r:r + 1], Skv[:], ALU.mult, ALU.add),
                reads=r_kv + [("pm",), ("Skv",)], writes=[("Skv",)])
            OPS("vector", lambda e, r=r, st_ks=st_ks: e.scalar_tensor_tensor(
                Sks[:], st_ks, pm_sb[:, r:r + 1], Sks[:], ALU.mult, ALU.add),
                reads=r_ks + [("pm",), ("Sks",)], writes=[("Sks",)])
        OPS("vector", lambda e: e.tensor_copy(Skv_bf[:], Skv[:]),
            reads=[("Skv",)], writes=[("Skvb",)])
        OPS("vector", lambda e: e.tensor_copy(Sks_bf[:], Sks[:]),
            reads=[("Sks",)], writes=[("Sksb",)])

    proj_fm(wq_d, BQ, NBQ, ("c_bq",), ("c_nbq",), qf_fm, rB, "qf",
            hook=ag2_prefix_hook)

    if DEBUG:
        for ic in range(NI):
            OPS("gpsimd", lambda e, ic=ic: e.dma_start(
                out=dbg["d_qf"][ic * P:(ic + 1) * P, :], in_=qf_fm[ic]),
                reads=[("qf", ic, 0), ("qf", ic, 1), rB(ic)], writes=[("dbgq",)],
                is_dma=True)
            OPS("gpsimd", lambda e, ic=ic: e.dma_start(
                out=dbg["d_kf"][ic * P:(ic + 1) * P, :], in_=kf_fm[ic]),
                reads=[("kf", ic, 0), ("kf", ic, 1), rCf(ic)], writes=[("dbgk",)],
                is_dma=True)
        for tt in range(NT):
            OPS("gpsimd", lambda e, tt=tt: e.dma_start(
                out=dbg["d_v"][tt * P:(tt + 1) * P, :], in_=v_tm[tt]),
                reads=[("vtm", tt, q) for q in range(4)] + [rD(tt)], writes=[("dbgv",)],
                is_dma=True)

    emit_ag1()

    for n in range(NT):
        for h in range(H):
            b = h // 4
            col = (h % 4) * P
            OPS("tensor", lambda e, h=h, n=n, b=b, col=col: e.matmul(
                PS[:, b * 512 + col:b * 512 + col + P],
                kf_tm[n][:, h * P:(h + 1) * P], v_tm[n][:, h * P:(h + 1) * P],
                start=True, stop=True),
                reads=[("kt", n, h), ("vtm", n, h // 4), rCt(n), rD(n)],
                writes=[psq(b, h % 4)])
        for h in range(H):
            OPS("tensor", lambda e, h=h, n=n: e.matmul(
                PS[:, 4 * 512 + h:4 * 512 + h + 1],
                kf_tm[n][:, h * P:(h + 1) * P], ones_col[:],
                start=True, stop=True),
                reads=[("kt", n, h), ("ones_c",), rCt(n)],
                acquire=[psq(4, 0)] if h == 0 else (),
                commit=[psq(4, 0)] if h == H - 1 else ())
        def pe_A(h):
            par = h % 2
            OPS("tensor", lambda e, h=h, n=n, par=par: e.matmul(
                PS[:, 5 * 512 + par * P:5 * 512 + (par + 1) * P],
                kf_fm[h][:, n * P:(n + 1) * P], qf_fm[h][:, n * P:(n + 1) * P],
                start=True, stop=True),
                reads=[("kf", h, n // 4), ("qf", h, n // 4), rCf(h), rB(h)],
                writes=[psq(5, par)])

        def dve_mask(h):
            par = h % 2
            OPS("vector", lambda e, par=par: e.tensor_mul(
                Asb[par], PS[:, 5 * 512 + par * P:5 * 512 + (par + 1) * P],
                mask_f[:]),
                reads=[psq(5, par), ("mask",)], writes=[("Asb", par)])

        def pe_I(h):
            par = h % 2
            OPS("tensor", lambda e, h=h, n=n, par=par: e.matmul(
                PS[:, 6 * 512 + par * P:6 * 512 + (par + 1) * P],
                Asb[par], v_tm[n][:, h * P:(h + 1) * P],
                start=True, stop=False),
                reads=[("Asb", par), ("vtm", n, h // 4), rD(n)],
                acquire=[psq(6, par)])
            OPS("tensor", lambda e, h=h, n=n, par=par: e.matmul(
                PS[:, 6 * 512 + par * P:6 * 512 + (par + 1) * P],
                qf_fm[h][:, n * P:(n + 1) * P], Skv_bf[:, h * P:(h + 1) * P],
                start=False, stop=True),
                reads=[("qf", h, n // 4), ("Skvb",), rB(h)],
                commit=[psq(6, par)])
            OPS("tensor", lambda e, h=h, n=n: e.matmul(
                PS[:, 4 * 512 + 16 + h:4 * 512 + 17 + h],
                Asb[h % 2], ones_col[:], start=True, stop=False),
                reads=[("Asb", h % 2), ("ones_c",)], acquire=[("ps4z", h)])
            OPS("tensor", lambda e, h=h, n=n: e.matmul(
                PS[:, 4 * 512 + 16 + h:4 * 512 + 17 + h],
                qf_fm[h][:, n * P:(n + 1) * P], Sks_bf[:, h:h + 1],
                start=False, stop=True),
                reads=[("qf", h, n // 4), ("Sksb",), rB(h)],
                commit=[("ps4z", h)])

        def act_zeps(h):
            OPS("scalar", lambda e, h=h: e.activation(
                sc(ZEPS + h), PS[:, 4 * 512 + 16 + h:4 * 512 + 17 + h],
                AF.Identity, bias=scl[:, 123:124]),
                reads=[("ps4z", h), ("c_eps",)], writes=[("zeps", h)])

        def dve_div(h):
            par = h % 2
            OPS("vector", lambda e, h=h: e.reciprocal(sc(ZR + h), sc(ZEPS + h)),
                reads=[("zeps", h)], writes=[("zr", h)])
            OPS("vector", lambda e, h=h, par=par: e.tensor_scalar(
                attm[par], PS[:, 6 * 512 + par * P:6 * 512 + (par + 1) * P],
                sc(ZR + h), None, ALU.mult),
                reads=[psq(6, par), ("zr", h)], writes=[("attm", par)])

        def pe_T(h):
            par = h % 2
            OPS("tensor", lambda e, par=par: e.transpose(
                PS_bf[:, 7 * 1024 + par * 256:7 * 1024 + par * 256 + P],
                attm[par], ident_bf[:]),
                reads=[("attm", par), ("ident",)], writes=[psq(7, par)])

        def act_evacA(h):
            par = h % 2
            OPS("scalar", lambda e, h=h, n=n, par=par: e.copy(
                att_fm[h][:, n * P:(n + 1) * P],
                PS_bf[:, 7 * 1024 + par * 256:7 * 1024 + par * 256 + P]),
                reads=[psq(7, par)], writes=[rA(h), ("af", h, n)])

        pe_A(0); dve_mask(0)
        pe_A(1); dve_mask(1)
        for h in range(H):
            pe_I(h)
            act_zeps(h)
            if h + 2 < H:
                pe_A(h + 2)
                dve_mask(h + 2)
            dve_div(h)
            pe_T(h)
            act_evacA(h)
        for b4 in range(4):
            OPS("vector", lambda e, b4=b4: e.tensor_add(
                Skv[:, b4 * 512:(b4 + 1) * 512],
                Skv[:, b4 * 512:(b4 + 1) * 512], bank(b4)),
                reads=psB(b4) + [("Skv",)], writes=[("Skv",)])
        OPS("vector", lambda e: e.tensor_add(
            Sks[:], Sks[:], PS[:, 4 * 512:4 * 512 + H]),
            reads=[psq(4, 0), ("Sks",)], writes=[("Sks",)])
        OPS("vector", lambda e: e.tensor_copy(Skv_bf[:], Skv[:]),
            reads=[("Skv",)], writes=[("Skvb",)])
        OPS("vector", lambda e: e.tensor_copy(Sks_bf[:], Sks[:]),
            reads=[("Sks",)], writes=[("Sksb",)])

    if DEBUG:
        for ic in range(NI):
            OPS("gpsimd", lambda e, ic=ic: e.dma_start(
                out=dbg["d_att"][ic * P:(ic + 1) * P, :], in_=att_fm[ic]),
                reads=[("af", ic, n2) for n2 in range(NT)] + [rA(ic)],
                writes=[("dbga",)], is_dma=True)

    OPS("gpsimd", lambda e: e.dma_start(
        out=brow[:], in_=bo_d.unsqueeze(0)),
        writes=[("brow",)], is_dma=True)
    xs_stage = [regX[:, 0:512], regX[:, 512:1024]]
    for oq in range(4):
        for blk in range(0, NI, 4):
            for ic in range(blk, blk + 4):
                load_wslice(wo_d, ic, oq, ic % 4, group=("wow", oq, blk))
            for ic in range(blk, blk + 4):
                par = ic % 4
                first = ic == 0
                for tt in range(NT):
                    OPS("tensor", lambda e, ic=ic, tt=tt, par=par: e.matmul(
                        bank(tt), att_fm[ic][:, tt * P:(tt + 1) * P], wbuf[par],
                        start=(ic == 0), stop=False),
                        reads=[("af", ic, tt), ("wbuf", par), rA(ic)],
                        acquire=psB(tt) if first else ())
        for tt in range(NT):
            OPS("tensor", lambda e, oq=oq, tt=tt: e.matmul(
                bank(tt), ones_row[:], brow[:, oq * 512:(oq + 1) * 512],
                start=False, stop=True),
                reads=[("ones_r",), ("brow",)], commit=psB(tt))
        for tt in range(NT):
            par = tt % 2
            OPS("sync", lambda e, oq=oq, tt=tt, par=par: e.dma_start(
                out=xs_stage[par],
                in_=x_sl[tt * P:(tt + 1) * P, oq * 512:(oq + 1) * 512]),
                writes=rXb(par * 512, par * 512 + 512), is_dma=True)
            OPS("vector", lambda e, oq=oq, tt=tt, par=par: e.tensor_add(
                x1n[tt][:, oq * 512:(oq + 1) * 512], xs_stage[par], bank(tt)),
                reads=rXb(par * 512, par * 512 + 512) + psB(tt),
                writes=[rC_x1(tt, oq), ("x1", tt, oq)])
            OPS("vector", lambda e, oq=oq, tt=tt: e.bn_stats(
                regX[:, 3072 + (tt * 4 + oq) * 6:3078 + (tt * 4 + oq) * 6],
                x1n[tt][:, oq * 512:(oq + 1) * 512]),
                reads=[("x1", tt, oq)],
                writes=[("bn8", tt, oq)] + rXb(3072 + tt * 24, 3096 + tt * 24))

    for tt in range(NT):
        rx1 = [("x1", tt, q) for q in range(4)]
        OPS("vector", lambda e, tt=tt: e.bn_aggr(
            sc(BNA, 2), regX[:, 3072 + tt * 24:3096 + tt * 24]),
            reads=[("bn8", tt, q) for q in range(4)] +
                  rXb(3072 + tt * 24, 3096 + tt * 24),
            writes=[("c_bna",)])
        OPS("vector", lambda e: e.tensor_scalar(
            sc(LNV), sc(BNA + 1), EPS_LN, None, ALU.add),
            reads=[("c_bna",)], writes=[("c_lnv",)])
        OPS("scalar", lambda e: e.activation(sc(LNV), sc(LNV), AF.Ln),
            reads=[("c_lnv",)], writes=[("c_lnv",)])
        OPS("scalar", lambda e: e.activation(
            sc(LNR), sc(LNV), AF.Exp, scale=-0.5),
            reads=[("c_lnv",)], writes=[("c_lnr",)])
        OPS("vector", lambda e: e.scalar_tensor_tensor(
            sc(LNB), sc(BNA), -1.0, sc(LNR), ALU.mult, ALU.mult),
            reads=[("c_bna",), ("c_lnr",)], writes=[("c_lnb",)])
        OPS("scalar", lambda e, tt=tt: e.activation(
            x1n[tt], x1n[tt], AF.Identity, bias=sc(LNB), scale=sc(LNR)),
            reads=rx1 + [("c_lnb",), ("c_lnr",)],
            writes=[("x1", tt, q) for q in range(4)] +
                   [rC_x1(tt, q) for q in range(4)])
        OPS("vector", lambda e, tt=tt: e.tensor_reduce(
            sc(AMX), x1n[tt], AX, ALU.max, apply_absolute_value=True),
            reads=rx1, writes=[("c_amx",)])
        OPS("vector", lambda e: e.tensor_scalar(
            sc(AMX), sc(AMX), EPS_FFN, None, ALU.max),
            reads=[("c_amx",)], writes=[("c_amx",)])
        OPS("vector", lambda e: e.reciprocal(sc(SFC), sc(AMX)),
            reads=[("c_amx",)], writes=[("c_sfc",)])
        OPS("vector", lambda e: e.tensor_scalar(
            sc(SFC), sc(SFC), 127.0, None, ALU.mult),
            reads=[("c_sfc",)], writes=[("c_sfc",)])
        OPS("vector", lambda e: e.reciprocal(sc(ISF), sc(SFC)),
            reads=[("c_sfc",)], writes=[("c_isf",)])
        for hh in range(2):
            sl = slice(hh * 1024, (hh + 1) * 1024)
            ti = regX_i32[:, 0:1024]
            e_ = regX[:, 1024:2048]
            f_ = regX[:, 2048:3072]
            OPS("vector", lambda e, tt=tt, sl=sl, ti=ti: e.tensor_scalar(
                ti, x1n[tt][:, sl], sc(SFC), None, ALU.mult),
                reads=rx1 + [("c_sfc",)], writes=rXb(0, 1024))
            OPS("vector", lambda e, tt=tt, sl=sl, ti=ti, e_=e_:
                e.scalar_tensor_tensor(
                    e_, x1n[tt][:, sl], sc(SFC), ti, ALU.mult, ALU.subtract),
                reads=rx1 + [("c_sfc",)] + rXb(0, 1024),
                writes=rXb(1024, 2048))
            OPS("vector", lambda e, e_=e_, f_=f_: e.tensor_scalar(
                f_, e_, -0.5, None, ALU.is_lt),
                reads=rXb(1024, 2048), writes=rXb(2048, 3072))
            OPS("vector", lambda e, ti=ti, e_=e_: e.scalar_tensor_tensor(
                e_, e_, 0.5, ti, ALU.is_gt, ALU.add),
                reads=rXb(1024, 2048) + rXb(0, 1024), writes=rXb(1024, 2048))
            OPS("vector", lambda e, e_=e_, f_=f_: e.tensor_sub(
                e_, e_, f_),
                reads=rXb(1024, 2048) + rXb(2048, 3072),
                writes=rXb(1024, 2048))
            OPS("scalar", lambda e, tt=tt, sl=sl, e_=e_: e.mul(
                xq_tm[tt][:, sl], e_, sc(ISF)),
                reads=rXb(1024, 2048) + [("c_isf",)],
                writes=[rD(tt), ("xqt", tt, sl.start)])
        OPS("sync", lambda e, tt=tt: e.dma_start(
            out=xq_dr[tt * P:(tt + 1) * P, :], in_=xq_tm[tt]),
            reads=[("xqt", tt, 0), ("xqt", tt, 1024), rD(tt)],
            writes=[("xqd", tt)], is_dma=True)
        if DEBUG:
            OPS("sync", lambda e, tt=tt: e.dma_start(
                out=dbg["d_x1n"][tt * P:(tt + 1) * P, :], in_=x1n[tt]),
                reads=[("x1", tt, q) for q in range(4)] +
                      [rC_x1(tt, q) for q in range(4)],
                writes=[("dbgx1",)], is_dma=True)
            OPS("gpsimd", lambda e, tt=tt: e.dma_start(
                out=dbg["d_xq"][tt * P:(tt + 1) * P, :], in_=xq_tm[tt]),
                reads=[("xqt", tt, 0), ("xqt", tt, 1024), rD(tt)], writes=[("dbgxq",)], is_dma=True)

    for ic in range(NI):
        OPS("sync", lambda e, ic=ic: e.dma_start_transpose(
            xq_fm[ic], xq_dr[:, ic * P:(ic + 1) * P]),
            reads=[("xqd", tt2) for tt2 in range(NT)],
            writes=[rB(ic), ("xqf", ic)], is_dma=True, group=("xqfm",))

    for os in range(len(OS_W)):
        w = OS_W[os]; o0 = OS_OFF[os]
        par = os % 2
        gres, ures = [], []
        for ic in range(NI):
            r = ic // 2
            res_g = ("f1wg", par, ic)
            OPS("sync", lambda e, ic=ic, r=r, o0=o0, w=w, par=par:
                e.dma_start_transpose(
                    wgbuf(par, ic)[:, 0:w],
                    wag_g_out[r * GU_ROWS + o0:r * GU_ROWS + o0 + w,
                              (ic % 2) * P:(ic % 2 + 1) * P]),
                reads=[("wag_out",)], writes=[rWg(par, ic), res_g],
                is_dma=True, group=("f1w", os))
            gres.append(res_g)
            res_u = ("f1wu", par, ic)
            OPS("sync", lambda e, ic=ic, r=r, o0=o0, w=w, par=par:
                e.dma_start_transpose(
                    wubuf(par, ic)[:, 0:w],
                    wag_g_out[r * GU_ROWS + DFF + o0:r * GU_ROWS + DFF + o0 + w,
                              (ic % 2) * P:(ic % 2 + 1) * P]),
                reads=[("wag_out",)], writes=[rWu(par, ic), res_u],
                is_dma=True, group=("f1w", os))
            ures.append(res_u)
        for half in range(2):
            tts = range(half * 4, half * 4 + 4)
            for ic in range(NI):
                for tt in tts:
                    bg = tt % 4; bu = 4 + tt % 4
                    first = ic == 0; last = ic == NI - 1
                    OPS("tensor",
                        lambda e, ic=ic, tt=tt, par=par, bg=bg, w=w: e.matmul(
                            bank(bg, w), xq_fm[ic][:, tt * P:(tt + 1) * P],
                            wgbuf(par, ic)[:, 0:w],
                            start=(ic == 0), stop=(ic == NI - 1)),
                        reads=[("xqf", ic), gres[ic], rWg(par, ic), rB(ic)],
                        acquire=psB(bg, w) if first else (),
                        commit=psB(bg, w) if last else ())
                    OPS("tensor",
                        lambda e, ic=ic, tt=tt, par=par, bu=bu, w=w: e.matmul(
                            bank(bu, w), xq_fm[ic][:, tt * P:(tt + 1) * P],
                            wubuf(par, ic)[:, 0:w],
                            start=(ic == 0), stop=(ic == NI - 1)),
                        reads=[("xqf", ic), ures[ic], rWu(par, ic), rB(ic)],
                        acquire=psB(bu, w) if first else (),
                        commit=psB(bu, w) if last else ())
            for tt in tts:
                bg = tt % 4; bu = 4 + tt % 4
                te, rte = (t1, ("t1",)) if tt % 2 == 0 else (t2, ("t2",))
                to, rto = (t3, ("t3",)) if tt % 2 == 0 else (t1, ("t1b",))
                if tt % 2 == 1:
                    rto = ("t1",)
                uc0 = 1376 + (tt % 2) * 512
                ucp = regX[:, uc0:uc0 + 512]
                rucp = [("f1u", tt % 2)] + rXb(uc0, uc0 + 512)
                OPS("scalar", lambda e, bg=bg, w=w, te=te: e.activation(
                    te[:, 0:w], bank(bg, w), AF.Silu),
                    reads=psB(bg, w), writes=[rte])
                OPS("scalar", lambda e, bu=bu, w=w, ucp=ucp: e.copy(
                    ucp[:, 0:w], bank(bu, w)),
                    reads=psB(bu, w), writes=rucp)
                OPS("vector", lambda e, w=w, te=te, to=to, ucp=ucp:
                    e.tensor_mul(to[:, 0:w], te[:, 0:w], ucp[:, 0:w]),
                    reads=[rte] + rucp, writes=[rto])
                OPS("vector", lambda e, w=w, to=to: e.tensor_reduce(
                    sc(AMXT), to[:, 0:w], AX, ALU.max,
                    apply_absolute_value=True),
                    reads=[rto], writes=[("c_amxt",)])
                OPS("vector", lambda e, tt=tt: e.tensor_max(
                    hamax[:, tt:tt + 1], hamax[:, tt:tt + 1], sc(AMXT)),
                    reads=[("c_amxt",), ("ham", tt)], writes=[("ham", tt)])
                OPS("scalar", lambda e, tt=tt, o0=o0, w=w, to=to: e.dma_start(
                    out=hid_dr[tt * P:(tt + 1) * P, o0:o0 + w], in_=to[:, 0:w]),
                    reads=[rto], writes=[("hidd", tt, os)], is_dma=True)

    regD_f32 = regD.bitcast(F32)
    f2_e_ = regD_f32[:, 3072:4448]
    r_f2e = [("rD", 3), ("rD", 4)]
    f2_hidq = [
        (regD_f32[:, 4448:5824], [("rD", 4), ("rD", 5)]),
        (regD_f32[:, 5824:7200], [("rD", 5), ("rD", 6), ("rD", 7)]),
        (regB_f32[:, 5120:6496], [("rB", 10), ("rB", 11), ("rB", 12)]),
    ]
    f2_hqq = [
        (regB[:, 12992:14368], [("rB", 12), ("rB", 13)]),
        (regB[:, 14368:15744], [("rB", 14), ("rB", 15)]),
    ]
    f2_ti = regX_i32[:, 0:1376]
    f2_f_ = regX[:, 1376:2752]
    HAM2S = [HAM2, 124]
    SF2S = [SF2, 125]
    IS2S = [IS2, 126]
    _f2j = [0]

    def f2_job(tt, q):
        j = _f2j[0]; _f2j[0] += 1
        hidq, rh = f2_hidq[j % 3]
        hqq, rq = f2_hqq[j % 2]
        sH, sS, sI = HAM2S[j % 2], SF2S[j % 2], IS2S[j % 2]
        q0 = q * 1376
        os_lo = q0 // 512
        os_hi = (q0 + 1375) // 512
        OPS("gpsimd", lambda e, tt=tt, q0=q0, hidq=hidq: e.dma_start(
            out=hidq, in_=hid_dr[tt * P:(tt + 1) * P, q0:q0 + 1376]),
            reads=[("hidd", tt, os2) for os2 in range(os_lo, os_hi + 1)],
            writes=rh, is_dma=True)
        OPS("vector", lambda e, tt=tt, sH=sH: e.tensor_scalar(
            sc(sH), hamax[:, tt:tt + 1], EPS_FFN, None, ALU.max),
            reads=[("ham", tt)], writes=[("c_ham2", sH)])
        OPS("vector", lambda e, sH=sH, sS=sS: e.reciprocal(sc(sS), sc(sH)),
            reads=[("c_ham2", sH)], writes=[("c_sf2", sS)])
        OPS("vector", lambda e, sS=sS: e.tensor_scalar(
            sc(sS), sc(sS), 127.0, None, ALU.mult),
            reads=[("c_sf2", sS)], writes=[("c_sf2", sS)])
        OPS("vector", lambda e, sS=sS, sI=sI: e.reciprocal(sc(sI), sc(sS)),
            reads=[("c_sf2", sS)], writes=[("c_is2", sI)])
        OPS("vector", lambda e, hidq=hidq, sS=sS: e.tensor_scalar(
            f2_ti, hidq, sc(sS), None, ALU.mult),
            reads=rh + [("c_sf2", sS)], writes=rXb(0, 1376))
        OPS("vector", lambda e, hidq=hidq, sS=sS: e.scalar_tensor_tensor(
            f2_e_, hidq, sc(sS), f2_ti, ALU.mult, ALU.subtract),
            reads=rh + rXb(0, 1376) + [("c_sf2", sS)], writes=r_f2e)
        OPS("vector", lambda e: e.tensor_scalar(
            f2_f_, f2_e_, -0.5, None, ALU.is_lt),
            reads=r_f2e, writes=rXb(1376, 2752))
        OPS("vector", lambda e: e.scalar_tensor_tensor(
            f2_e_, f2_e_, 0.5, f2_ti, ALU.is_gt, ALU.add),
            reads=r_f2e + rXb(0, 1376), writes=r_f2e)
        OPS("vector", lambda e: e.tensor_sub(f2_e_, f2_e_, f2_f_),
            reads=r_f2e + rXb(1376, 2752), writes=r_f2e)
        OPS("scalar", lambda e, hqq=hqq, sI=sI: e.mul(hqq, f2_e_, sc(sI)),
            reads=r_f2e + [("c_is2", sI)], writes=rq + [("f2hq", j % 2)])
        OPS("scalar", lambda e, tt=tt, q0=q0, hqq=hqq, j=j: e.dma_start(
            out=hq_dr[tt * P:(tt + 1) * P, q0:q0 + 1376], in_=hqq),
            reads=rq + [("f2hq", j % 2)],
            writes=[("hqd", tt), ("hqd2", tt, q)], is_dma=True)

    def emit_hqfm(half, qlevel):
        hrow0 = half * 512
        tts = list(range(half * 4, half * 4 + 4))
        for fc in range(NF):
            if (fc * P + P - 1) // 1376 != qlevel:
                continue
            OPS("sync", lambda e, fc=fc, hrow0=hrow0: e.dma_start_transpose(
                hq_fm(fc), hq_dr[hrow0:hrow0 + 512, fc * P:(fc + 1) * P]),
                reads=[("hqd2", tt2, qq) for tt2 in tts
                       for qq in range(qlevel + 1)],
                writes=[rHq(fc), ("hqf", fc)], is_dma=True,
                group=("hqfm", half, qlevel))

    def f3_half(half, post_pair=None):
        tts = list(range(half * 4, half * 4 + 4))
        emitted_lvl = [-1]

        def need_lvl(lvl):
            while emitted_lvl[0] < lvl:
                emitted_lvl[0] += 1
                emit_hqfm(half, emitted_lvl[0])

        for pair in range(2):
            for gidx, fb2 in enumerate(range(0, NF, 2)):
                fcs = [f for f in (fb2, fb2 + 1) if f < NF]
                if pair == 0:
                    look = min(fb2 + 3, NF - 1)
                    need_lvl((look * P + P - 1) // 1376)
                for fc in fcs:
                    for oq2 in range(2):
                        oq4 = pair * 2 + oq2
                        par = 2 * (fc - fb2) + oq2
                        rr = next(r for r in range(RNK)
                                  if WD_RANGES[r][0] <= fc < WD_RANGES[r][1])
                        lcol = (fc - WD_RANGES[rr][0]) * P
                        OPS("sync", lambda e, oq4=oq4, par=par, rr=rr,
                            lcol=lcol: e.dma_start_transpose(
                                wdbuf[par],
                                wag_d_out[rr * DM + oq4 * 512:
                                          rr * DM + (oq4 + 1) * 512,
                                          lcol:lcol + P]),
                            reads=[("wag_outd",)],
                            writes=[("wdb", par), ("rB", 8 + par // 2)],
                            is_dma=True, group=("wdw", half, pair, fb2))
                for fc in fcs:
                    for oq2 in range(2):
                        par = 2 * (fc - fb2) + oq2
                        for tt in tts:
                            b = (tt % 4) * 2 + oq2
                            OPS("tensor",
                                lambda e, fc=fc, tt=tt, par=par, b=b: e.matmul(
                                    bank(b),
                                    hq_fm(fc)[:, (tt % 4) * P:(tt % 4 + 1) * P],
                                    wdbuf[par],
                                    start=(fc == 0), stop=(fc == NF - 1)),
                                reads=[("hqf", fc), ("wdb", par),
                                       ("rB", 8 + par // 2), rHq(fc)],
                                acquire=psB(b) if fc == 0 else (),
                                commit=psB(b) if fc == NF - 1 else ())
            for tt in tts:
                for oq2 in range(2):
                    b = (tt % 4) * 2 + oq2
                    oq4 = pair * 2 + oq2
                    OPS("vector", lambda e, tt=tt, oq4=oq4, b=b: e.tensor_add(
                        x1n[tt][:, oq4 * 512:(oq4 + 1) * 512],
                        x1n[tt][:, oq4 * 512:(oq4 + 1) * 512], bank(b)),
                        reads=psB(b) + [("x1", tt, oq4)],
                        writes=[("x1", tt, oq4), rC_x1(tt, oq4)])
            if post_pair is not None:
                post_pair(pair)
        for tt in tts:
            rx2 = [("x1", tt, q) for q in range(4)]
            for q4 in range(4):
                OPS("vector", lambda e, tt=tt, q4=q4: e.bn_stats(
                    bnst[:, q4, :], x1n[tt][:, q4 * 512:(q4 + 1) * 512]),
                    reads=[("x1", tt, q4)], writes=[("bnst", q4)])
            OPS("vector", lambda e: e.bn_aggr(
                sc(BNA, 2), bnst.rearrange("p a b -> p (a b)")),
                reads=[("bnst", q) for q in range(4)], writes=[("c_bna",)])
            OPS("vector", lambda e: e.tensor_scalar(
                sc(LNV), sc(BNA + 1), EPS_LN, None, ALU.add),
                reads=[("c_bna",)], writes=[("c_lnv",)])
            OPS("scalar", lambda e: e.activation(sc(LNV), sc(LNV), AF.Ln),
                reads=[("c_lnv",)], writes=[("c_lnv",)])
            OPS("scalar", lambda e: e.activation(
                sc(LNR), sc(LNV), AF.Exp, scale=-0.5),
                reads=[("c_lnv",)], writes=[("c_lnr",)])
            OPS("vector", lambda e: e.scalar_tensor_tensor(
                sc(LNB), sc(BNA), -1.0, sc(LNR), ALU.mult, ALU.mult),
                reads=[("c_bna",), ("c_lnr",)], writes=[("c_lnb",)])
            par = tt % 2
            ot = out_tile[par]
            OPS("scalar", lambda e, tt=tt, ot=ot: e.activation(
                ot, x1n[tt], AF.Identity, bias=sc(LNB), scale=sc(LNR)),
                reads=rx2 + [("c_lnb",), ("c_lnr",)],
                writes=rOut(par) + [("outt", par)])
            OPS("scalar", lambda e, tt=tt, ot=ot: e.dma_start(
                out=out_sl[tt * P:(tt + 1) * P, :], in_=ot),
                reads=[("outt", par)], writes=[("outd", tt)], is_dma=True)

    if DEBUG:
        for tt in range(NT):
            OPS("gpsimd", lambda e, tt=tt: e.dma_start(
                out=dbg["d_hid"][tt * P:(tt + 1) * P, :],
                in_=hid_dr[tt * P:(tt + 1) * P, :]),
                reads=[("hidd", tt, os2) for os2 in range(len(OS_W))],
                writes=[("dbghid",)], is_dma=True)
    for q in range(4):
        for tt in range(4):
            f2_job(tt, q)

    def post0(pair):
        lvls = (0, 1) if pair == 0 else (2, 3)
        for lvl in lvls:
            for tt in range(4, 8):
                f2_job(tt, lvl)

    f3_half(0, post0)
    if DEBUG:
        for tt in range(NT):
            OPS("gpsimd", lambda e, tt=tt: e.dma_start(
                out=dbg["d_hq"][tt * P:(tt + 1) * P, :],
                in_=hq_dr[tt * P:(tt + 1) * P, :]),
                reads=[("hqd", tt)], writes=[("dbghq",)], is_dma=True)
    f3_half(1)

    OPS("gpsimd", lambda e: e.engine_nop(),
        reads=[("outd", tt) for tt in range(NT)] +
              ([("dbgq",), ("dbgk",), ("dbgv",), ("dbga",), ("dbgx1",),
                ("dbgxq",), ("dbg_kv",), ("dbgwg",), ("dbghid",), ("dbghq",)]
               if DEBUG else []))

    B.finalize(nc, ctx)
    with nc.Block() as block:
        B.replay(block)
    ctx.close()
    return nc


def _contig(a):
    return np.ascontiguousarray(np.asarray(a))


def kernel(L0_attn_x, L0_attn_Wq, L0_attn_bq, L0_attn_Wk, L0_attn_bk,
           L0_attn_Wv, L0_attn_bv, L0_attn_Wo, L0_attn_bo,
           L0_ffn_w_gate, L0_ffn_w_up, L0_ffn_w_down):
    x = np.asarray(L0_attn_x)
    wg = np.asarray(L0_ffn_w_gate); wu = np.asarray(L0_ffn_w_up)
    wd = np.asarray(L0_ffn_w_down)
    nc = build_nc()
    in_maps = []
    for c in range(RNK):
        b, p = c // 4, c % 4
        f0, f1 = WD_RANGES[c]
        wd_pad = np.zeros((DM, WD_PAD * P), np.float32)
        wd_pad[:, 0:(f1 - f0) * P] = wd[:, f0 * P:f1 * P]
        pm = np.zeros((P, RNK), np.float32)
        for r in range(RNK):
            if r // 4 == b and r % 4 < p:
                pm[:, r] = 1.0
        in_maps.append({
            "x_sl": _contig(x[b, p * T:(p + 1) * T]),
            "wq": _contig(L0_attn_Wq), "wk": _contig(L0_attn_Wk),
            "wv": _contig(L0_attn_Wv), "wo": _contig(L0_attn_Wo),
            "bq": _contig(L0_attn_bq), "bk": _contig(L0_attn_bk),
            "bv": _contig(L0_attn_bv), "bo": _contig(L0_attn_bo),
            "wg_sh": _contig(wg[:, 256 * c:256 * (c + 1)]),
            "wu_sh": _contig(wu[:, 256 * c:256 * (c + 1)]),
            "wd_sh": wd_pad,
            "pm": pm,
        })
    res = run_bass_kernel_spmd(nc, in_maps, core_ids=list(range(RNK)))
    out = np.zeros((2, 4096, DM), np.float32)
    for c in range(RNK):
        b, p = c // 4, c % 4
        out[b, p * T:(p + 1) * T] = res.results[c]["out_sl"]
    return out


if __name__ == "__main__":
    nc = build_nc()
    print("build ok, instructions:", nc.n_instructions())


# version 24
# speedup vs baseline: 1.0030x; 1.0030x over previous
import numpy as np
import ml_dtypes
import concourse.bass as bass
import concourse.mybir as mybir
from concourse.bass_utils import run_bass_kernel_spmd
from contextlib import ExitStack

F32 = mybir.dt.float32
BF16 = mybir.dt.bfloat16
I32 = mybir.dt.int32
AF = mybir.ActivationFunctionType
ALU = mybir.AluOpType
AX = mybir.AxisListType.X

P = 128
T = 1024
NT = 8
H = 16
DM = 2048
NI = DM // P
DFF = 5504
NF = DFF // P
EPS_ATT = 1e-6
EPS_LN = 1e-5
EPS_FFN = 1e-5
RNK = 8
K17 = float(2.0 ** 17)
K17I = float(2.0 ** -17)

DEBUG = False

WD_RANGES = [(NF * c // RNK, NF * (c + 1) // RNK) for c in range(RNK)]
WD_PAD = 6
GU_ROWS = 2 * DFF
OS_W = [512] * 10 + [384]
OS_OFF = [sum(OS_W[:i]) for i in range(len(OS_W))]


class Op:
    __slots__ = ("eng", "fn", "is_dma", "idx", "deps", "tagged", "tag_val",
                 "waits", "stream", "fence_wait", "group")
    def __init__(self, eng, fn, is_dma, idx, stream):
        self.eng, self.fn, self.is_dma, self.idx = eng, fn, is_dma, idx
        self.deps = set(); self.tagged = False; self.tag_val = None
        self.waits = {}; self.stream = stream; self.fence_wait = None
        self.group = None

class Builder:
    def __init__(self):
        self.ops = []
        self.res = {}

    def _st(self, r):
        if r not in self.res:
            self.res[r] = {"w": None, "r": set()}
        return self.res[r]

    def op(self, eng, fn, reads=(), writes=(), acquire=(), commit=(),
           is_dma=False, stream=None, group=None):
        idx = len(self.ops)
        stream = stream or (eng, "d" if is_dma else "c")
        o = Op(eng, fn, is_dma, idx, stream)
        o.group = group
        self.ops.append(o)
        for r in reads:
            st = self._st(r)
            if st["w"] is not None:
                o.deps.add(st["w"])
            st["r"].add(idx)
        for r in writes:
            st = self._st(r)
            if st["w"] is not None:
                o.deps.add(st["w"])
            o.deps.update(st["r"])
            st["r"] = set(); st["w"] = idx
        for r in acquire:
            st = self._st(r)
            if st["w"] is not None:
                o.deps.add(st["w"])
            o.deps.update(st["r"])
            st["r"] = set(); st["w"] = None
        for r in commit:
            st = self._st(r)
            st["w"] = idx; st["r"] = set()
        o.deps.discard(idx)
        return idx

    def finalize(self, nc, ctx):
        ops = self.ops
        heads = {}
        for o in ops:
            if o.group is None:
                continue
            key = (o.group, o.stream)
            if key not in heads:
                heads[key] = o
            else:
                h = heads[key]
                kept = set()
                for d in o.deps:
                    if d < h.idx:
                        h.deps.add(d)
                    elif d != h.idx:
                        kept.add(d)
                o.deps = kept
        by_eng = {}
        for o in ops:
            by_eng.setdefault(o.eng, []).append(o)

        def compute_bursts():
            burst_of = {}
            burst_members = {}
            nb = {}
            for eng, olist in by_eng.items():
                open_bursts = {}
                for o in olist:
                    if o.deps:
                        open_bursts.clear()
                    if o.is_dma:
                        s = o.stream
                        if s not in open_bursts:
                            nb[s] = nb.get(s, -1) + 1
                            open_bursts[s] = nb[s]
                        bid = (s, open_bursts[s])
                        burst_of[o.idx] = bid
                        burst_members.setdefault(bid, []).append(o.idx)
            return burst_of, burst_members

        for _round in range(40):
            burst_of, burst_members = compute_bursts()
            changed = False
            for o in ops:
                for d in list(o.deps):
                    do = ops[d]
                    if not do.is_dma:
                        continue
                    mem = burst_members[burst_of[d]]
                    if mem[-1] > o.idx:
                        k = next(i for i in mem if i > o.idx)
                        prev = mem[mem.index(k) - 1]
                        if prev not in ops[k].deps:
                            ops[k].deps.add(prev)
                            changed = True
            if not changed:
                break
        else:
            raise RuntimeError("fence injection did not converge")

        for o in ops:
            if o.is_dma:
                o.tagged = True
            for d in o.deps:
                ops[d].tagged = True
        tagcount = {}
        for o in ops:
            k = o.stream
            c = tagcount.get(k, 0)
            if o.tagged:
                c += 1
                o.tag_val = c * (16 if o.is_dma else 1)
            tagcount[k] = c
        burst_end_val = {}
        for bid, mem in burst_members.items():
            burst_end_val[bid] = max(ops[i].tag_val for i in mem)
        for o in ops:
            w = {}
            for d in o.deps:
                do = ops[d]
                if do.is_dma:
                    v = burst_end_val[burst_of[d]]
                else:
                    v = do.tag_val
                w[do.stream] = max(w.get(do.stream, 0), v)
            o.waits = w
        for bid, mem in burst_members.items():
            first = ops[mem[0]]
            s = first.stream
            prior = first.tag_val - 16
            if prior > 0:
                first.waits[s] = max(first.waits.get(s, 0), prior)
                first.fence_wait = (s, prior)
        self.sems = {}
        for k in sorted({o.stream for o in self.ops}, key=str):
            self.sems[k] = ctx.enter_context(nc.semaphore(f"s_{k[0]}_{k[1]}"))
        return tagcount

    def replay(self, block):
        by_eng = {}
        for o in self.ops:
            by_eng.setdefault(o.eng, []).append(o)
        hw = {}
        def mk(eng_name, ops):
            def body(eng):
                for o in ops:
                    wl = sorted(o.waits.items(), key=lambda kv: str(kv[0]))
                    for k, v in wl:
                        hk = (eng_name, k)
                        forced = o.fence_wait is not None and o.fence_wait[0] == k
                        if not forced and hw.get(hk, -1) >= v:
                            continue
                        eng.wait_ge(self.sems[k], v)
                        hw[hk] = max(hw.get(hk, -1), v)
                    inst = o.fn(eng)
                    if o.tagged:
                        inst.then_inc(self.sems[o.stream], 16 if o.is_dma else 1)
            return body
        for eng_name, ops in by_eng.items():
            getattr(block, eng_name)(mk(eng_name, ops))


def build_nc():
    nc = bass.Bass("TRN2", target_bir_lowering=False)
    B = Builder()
    ctx = ExitStack()
    E = ctx.enter_context
    OPS = B.op

    ap_in = lambda n, s: nc.dram_tensor(n, s, F32, kind="ExternalInput").ap()
    x_sl = ap_in("x_sl", [T, DM])
    wq_d = ap_in("wq", [DM, DM]); wk_d = ap_in("wk", [DM, DM])
    wv_d = ap_in("wv", [DM, DM]); wo_d = ap_in("wo", [DM, DM])
    bq_d = ap_in("bq", [DM]); bk_d = ap_in("bk", [DM])
    bv_d = ap_in("bv", [DM]); bo_d = ap_in("bo", [DM])
    wg_sh = ap_in("wg_sh", [DFF, 256])
    wu_sh = ap_in("wu_sh", [DFF, 256])
    wd_sh = ap_in("wd_sh", [DM, WD_PAD * P])
    pm_d = ap_in("pm", [P, RNK])
    out_sl = nc.dram_tensor("out_sl", [T, DM], F32, kind="ExternalOutput").ap()

    xbf_dr = nc.dram_tensor("xbf_dr", [T, DM], BF16, kind="Internal").ap()
    xq_dr = nc.dram_tensor("xq_dr", [T, DM], BF16, kind="Internal").ap()
    hid_dr = nc.dram_tensor("hid_dr", [T, DFF], F32, kind="Internal").ap()
    hq_dr = nc.dram_tensor("hq_dr", [T, DFF], BF16, kind="Internal").ap()
    wag_g_in = nc.dram_tensor("wag_g_in", [GU_ROWS, 256], BF16, kind="Internal").ap()
    wag_g_out = nc.dram_tensor("wag_g_out", [RNK * GU_ROWS, 256], BF16, kind="Internal").ap()
    wag_d_in = nc.dram_tensor("wag_d_in", [DM, WD_PAD * P], BF16, kind="Internal").ap()
    wag_d_out = nc.dram_tensor("wag_d_out", [RNK * DM, WD_PAD * P], BF16, kind="Internal").ap()
    kag_in = nc.dram_tensor("kag_in", [P, DM + H], F32, kind="Internal").ap()
    kag_out = nc.dram_tensor("kag_out", [RNK * P, DM + H], F32, kind="Internal").ap()

    dbg = {}
    if DEBUG:
        for nm, shape in (("d_qf", [DM, T]), ("d_kf", [DM, T]), ("d_v", [T, DM]),
                          ("d_kvtot", [P, DM + H]), ("d_att", [DM, T]),
                          ("d_x1n", [T, DM]), ("d_xq", [T, DM]),
                          ("d_wgq", [GU_ROWS, 256]), ("d_hid", [T, DFF]),
                          ("d_hq", [T, DFF])):
            dbg[nm] = nc.dram_tensor(nm, shape, F32, kind="ExternalOutput").ap()

    regA = E(nc.sbuf_tensor("regA", [P, 16 * 1024], BF16))
    regB = E(nc.sbuf_tensor("regB", [P, 16 * 1024], BF16))
    regC = E(nc.sbuf_tensor("regC", [P, 32 * 1024], BF16))
    regD = E(nc.sbuf_tensor("regD", [P, 16 * 1024], BF16))
    regX = E(nc.sbuf_tensor("regX", [P, 3712], F32))
    Skv = E(nc.sbuf_tensor("Skv", [P, DM], F32))
    Sks = E(nc.sbuf_tensor("Sks", [P, H], F32))
    Skv_bf = E(nc.sbuf_tensor("Skv_bf", [P, DM], BF16))
    Sks_bf = E(nc.sbuf_tensor("Sks_bf", [P, H], BF16))
    brow = E(nc.sbuf_tensor("brow", [1, DM], BF16))
    mask_f = E(nc.sbuf_tensor("mask_f", [P, P], F32))
    ident_bf = E(nc.sbuf_tensor("ident_bf", [P, P], BF16))
    ones_col = E(nc.sbuf_tensor("ones_col", [P, 1], BF16))
    ones_row = E(nc.sbuf_tensor("ones_row", [1, P], BF16))
    t1 = E(nc.sbuf_tensor("t1", [P, 512], F32))
    t2 = E(nc.sbuf_tensor("t2", [P, 512], F32))
    t3 = E(nc.sbuf_tensor("t3", [P, 512], F32))
    wstage = E(nc.sbuf_tensor("wstage", [P, 768], F32))
    wbuf_t = E(nc.sbuf_tensor("wbuf", [P, 4, 512], BF16))
    wbuf = [wbuf_t[:, i, :] for i in range(4)]
    wdbuf = None
    Asb_t = E(nc.sbuf_tensor("Asb", [P, 2, P], BF16))
    Asb = [Asb_t[:, 0, :], Asb_t[:, 1, :]]
    attm_t = E(nc.sbuf_tensor("attm", [P, 2, P], BF16))
    attm = [attm_t[:, 0, :], attm_t[:, 1, :]]
    scl = E(nc.sbuf_tensor("scl", [P, P], F32))
    hamax = E(nc.sbuf_tensor("hamax", [P, NT], F32))
    pm_sb = E(nc.sbuf_tensor("pm_sb", [P, RNK], F32))
    bnst = E(nc.sbuf_tensor("bnst", [P, 4, 6], F32))
    PS = E(nc.psum_tensor("PS", [P, 4096], F32))

    BQ, BK, NBQ, NBK = 0, 16, 32, 48
    ZEPS, ZR = 64, 80
    QSC, QRI = 96, 104
    AMX, SFC, ISF = 111, 112, 113
    BNA, LNV, LNR, LNB = 114, 116, 117, 118
    AMXT, HAM2, SF2, IS2 = 119, 120, 121, 122
    sc = lambda c, n=1: scl[:, c:c + n]

    x_fm = [regA[:, ic * 1024:(ic + 1) * 1024] for ic in range(NI)]
    att_fm = x_fm
    rA = lambda ic: ("rA", ic)
    qf_fm = [regB[:, ic * 1024:(ic + 1) * 1024] for ic in range(NI)]
    xq_fm = qf_fm
    rB = lambda ic: ("rB", ic)
    kf_fm = [regC[:, ic * 1024:(ic + 1) * 1024] for ic in range(NI)]
    kf_tm = [regC[:, 16 * 1024 + tt * 2048:16 * 1024 + (tt + 1) * 2048]
             for tt in range(NT)]
    rCf = lambda ic: ("rC", ic)
    rCt = lambda tt: ("rC", 16 + tt)
    v_tm = [regD[:, tt * 2048:(tt + 1) * 2048] for tt in range(NT)]
    rD = lambda tt: ("rD", tt)
    regC_f32 = regC.bitcast(F32)
    x1n = [regC_f32[:, tt * 2048:(tt + 1) * 2048] for tt in range(NT)]
    rX1 = lambda tt, q: ("rC", "x1", tt, q)
    def rC_x1(tt, q):
        bf_block = (tt * 4096 + q * 1024) // 1024
        if bf_block < 16:
            return ("rC", bf_block)
        return ("rC", 16 + (bf_block - 16) // 2)
    wgbuf = lambda par, ic: regA[:, (par * NI + ic) * 512:(par * NI + ic + 1) * 512]
    rWg = lambda par, ic: ("rA", (par * NI + ic) // 2)
    wubuf = lambda par, ic: regD[:, (par * NI + ic) * 512:(par * NI + ic + 1) * 512]
    rWu = lambda par, ic: ("rD", (par * NI + ic) // 4)
    def hq_fm(fc):
        if fc < 32:
            return regA[:, fc * 512:(fc + 1) * 512]
        return regD[:, (fc - 32) * 512:(fc - 31) * 512]
    def rHq(fc):
        if fc < 32:
            return ("rA", fc // 2)
        return ("rD", (fc - 32) // 4)
    xq_tm = [regD[:, tt * 2048:(tt + 1) * 2048] for tt in range(NT)]
    regB_f32 = regB.bitcast(F32)
    out_tile = [regB_f32[:, 0:2048], regB_f32[:, 2048:4096]]
    wdbuf = [regB[:, 8 * 1024 + i * 512:8 * 1024 + (i + 1) * 512]
             for i in range(4)]
    rOut = lambda par: [("rB", 4 * par), ("rB", 4 * par + 1),
                        ("rB", 4 * par + 2), ("rB", 4 * par + 3)]
    regX_i32 = regX.bitcast(I32)
    regX_bf = regX.bitcast(BF16)
    rXb = lambda c0, c1: [("rX", b) for b in range(c0 // 512, (c1 + 511) // 512)]
    xc_tmp = [regB[:, 12 * 1024:12 * 1024 + 2048],
              regB[:, 14 * 1024:14 * 1024 + 2048]]
    rXc = [[("rB", 12), ("rB", 13)], [("rB", 14), ("rB", 15)]]

    bank = lambda b, w=512: PS[:, b * 512:b * 512 + w]
    PS_bf = PS.bitcast(BF16)
    psq = lambda b, q: ("ps", b, q)
    psB = lambda b, w=512: [("ps", b, q) for q in range((w + 127) // 128)]

    OPS("gpsimd", lambda e: e.memset(mask_f[:], 1.0), writes=[("mask",)])
    OPS("gpsimd", lambda e: e.affine_select(
        out=mask_f[:], in_=mask_f[:], compare_op=ALU.is_ge, fill=0.0,
        base=0, pattern=[[1, P]], channel_multiplier=-1),
        reads=[("mask",)], writes=[("mask",)])
    OPS("gpsimd", lambda e: e.memset(ident_bf[:], 1.0), writes=[("ident",)])
    OPS("gpsimd", lambda e: e.affine_select(
        out=ident_bf[:], in_=ident_bf[:], compare_op=ALU.is_equal, fill=0.0,
        base=0, pattern=[[1, P]], channel_multiplier=-1),
        reads=[("ident",)], writes=[("ident",)])
    OPS("vector", lambda e: e.memset(ones_col[:], 1.0), writes=[("ones_c",)])
    OPS("vector", lambda e: e.memset(ones_row[:], 1.0), writes=[("ones_r",)])
    OPS("vector", lambda e: e.memset(Skv[:], 0.0), writes=[("Skv",)])
    OPS("vector", lambda e: e.memset(Sks[:], 0.0), writes=[("Sks",)])
    OPS("vector", lambda e: e.memset(hamax[:], 0.0),
        writes=[("ham", tt) for tt in range(NT)])
    OPS("vector", lambda e: e.memset(scl[:, 123:124], EPS_ATT),
        writes=[("c_eps",)])
    OPS("sync", lambda e: e.dma_start(
        out=sc(BQ, NI), in_=bq_d.rearrange("(o p) -> p o", p=P),
        allow_slow_non_contiguous=True),
        writes=[("c_bq",)], is_dma=True)
    OPS("sync", lambda e: e.dma_start(
        out=sc(BK, NI), in_=bk_d.rearrange("(o p) -> p o", p=P),
        allow_slow_non_contiguous=True),
        writes=[("c_bk",)], is_dma=True)
    OPS("sync", lambda e: e.dma_start(out=pm_sb[:], in_=pm_d),
        writes=[("pm",)], is_dma=True)
    OPS("gpsimd", lambda e: e.dma_start(
        out=brow[:], in_=bv_d.unsqueeze(0)),
        writes=[("brow",)], is_dma=True)
    OPS("vector", lambda e: e.tensor_scalar(
        sc(NBQ, NI), sc(BQ, NI), -1.0, None, ALU.mult),
        reads=[("c_bq",)], writes=[("c_nbq",)])
    OPS("vector", lambda e: e.tensor_scalar(
        sc(NBK, NI), sc(BK, NI), -1.0, None, ALU.mult),
        reads=[("c_bk",)], writes=[("c_nbk",)])

    for tt in range(NT):
        buf = xc_tmp[tt % 2]
        rr = rXc[tt % 2]
        OPS("gpsimd", lambda e, tt=tt, buf=buf: e.dma_start(
            out=buf, in_=x_sl[tt * P:(tt + 1) * P, :]),
            writes=rr, is_dma=True)
        OPS("gpsimd", lambda e, tt=tt, buf=buf: e.dma_start(
            out=xbf_dr[tt * P:(tt + 1) * P, :], in_=buf),
            reads=rr, writes=[("xbf", tt)], is_dma=True)
    for ic in range(NI):
        OPS("sync", lambda e, ic=ic: e.dma_start_transpose(
            x_fm[ic], xbf_dr[:, ic * P:(ic + 1) * P]),
            reads=[("xbf", tt2) for tt2 in range(NT)], writes=[rA(ic)],
            is_dma=True, group=("xfm",))

    regB_f32q = regB.bitcast(F32)
    _qpar = [0]

    def quant_tile(src_ap, dst_ap, G):
        W = G * P
        pp = _qpar[0] & 1
        _qpar[0] += 1
        if pp == 0:
            stage = wstage[:, 0:W]
            rstage = [("qstage", 0)]
            qbf = regX_bf[:, 6400:6400 + W]
            rqbf = rXb(3200, 3200 + (W + 1) // 2)
        else:
            stage = regB_f32q[:, 0:W]
            rstage = [("qstage", 1), ("rB", 0), ("rB", 1)]
            qbf = regB[:, 1536:1536 + W]
            rqbf = [("rB", 1), ("rB", 2)]
        OPS("scalar", lambda e: e.dma_start(out=stage, in_=src_ap),
            writes=rstage, is_dma=True)
        aw = regX[:, 0:W]
        OPS("scalar", lambda e: e.activation(aw, stage, AF.Abs),
            reads=rstage, writes=rXb(0, W))
        hi_i = regX_i32[:, 768:768 + W]
        OPS("scalar", lambda e: e.activation(hi_i, aw, AF.Identity, scale=K17),
            reads=rXb(0, W), writes=rXb(768, 768 + W))
        lo = regX[:, 1536:1536 + W]
        OPS("vector", lambda e: e.scalar_tensor_tensor(
            lo, hi_i, -K17I, aw, ALU.mult, ALU.add),
            reads=rXb(768, 768 + W) + rXb(0, W), writes=rXb(1536, 1536 + W))
        rh = regX[:, 2304:2304 + G]
        rl = regX[:, 2320:2320 + G]
        OPS("vector", lambda e: e.tensor_reduce(
            rh, hi_i.rearrange("p (g e) -> p g e", g=G), AX, ALU.add),
            reads=rXb(768, 768 + W), writes=rXb(2304, 2310))
        OPS("vector", lambda e: e.tensor_reduce(
            rl, lo.rearrange("p (g e) -> p g e", g=G), AX, ALU.add),
            reads=rXb(1536, 1536 + W), writes=rXb(2320, 2326))
        OPS("vector", lambda e: e.scalar_tensor_tensor(
            sc(QSC, G), rh, K17I, rl, ALU.mult, ALU.add),
            reads=rXb(2304, 2326), writes=[("c_qsc",)])
        OPS("vector", lambda e: e.tensor_scalar(
            sc(QSC, G), sc(QSC, G), 1.0 / P, EPS_FFN, ALU.mult, ALU.add),
            reads=[("c_qsc",)], writes=[("c_qsc",)])
        OPS("vector", lambda e: e.reciprocal(sc(QRI, G), sc(QSC, G)),
            reads=[("c_qsc",)], writes=[("c_qri",)])
        sgn = regX[:, 1536:1536 + W]
        OPS("scalar", lambda e: e.activation(sgn, stage, AF.Sign),
            reads=rstage + rXb(1536, 1536 + W),
            writes=rXb(1536, 1536 + W) + [("rX_sgn",)])
        m01 = regX[:, 2432:2432 + W]
        for g in range(G):
            OPS("vector", lambda e, g=g: e.tensor_scalar(
                m01[:, g * P:(g + 1) * P], aw[:, g * P:(g + 1) * P],
                sc(QRI + g), 0.5, ALU.mult, ALU.is_gt),
                reads=rXb(0, W) + [("c_qri",)],
                writes=[("rX_m", g)] + (rXb(2432, 2432 + W) if g == 0 else []))
        for g in range(G):
            OPS("vector", lambda e, g=g: e.scalar_tensor_tensor(
                qbf[:, g * P:(g + 1) * P], m01[:, g * P:(g + 1) * P],
                sc(QSC + g), sgn[:, g * P:(g + 1) * P],
                ALU.mult, ALU.mult),
                reads=[("rX_m", g), ("rX_sgn",), ("c_qsc",)] +
                      rXb(1536, 1536 + W) + rXb(2432, 2432 + W),
                writes=[("rX_qbf", pp, g)] + (rqbf if g == 0 else []))
        OPS("sync", lambda e: e.dma_start(out=dst_ap, in_=qbf),
            reads=[("rX_qbf", pp, g) for g in range(G)] + rqbf,
            writes=[("wag_w",)], is_dma=True)

    quant_jobs = []
    for r in range(NF):
        quant_jobs.append((wg_sh[r * P:(r + 1) * P, :],
                           wag_g_in[r * P:(r + 1) * P, :], 2))
    for r in range(NF):
        quant_jobs.append((wu_sh[r * P:(r + 1) * P, :],
                           wag_g_in[DFF + r * P:DFF + (r + 1) * P, :], 2))
    for r in range(NI):
        quant_jobs.append((wd_sh[r * P:(r + 1) * P, :],
                           wag_d_in[r * P:(r + 1) * P, :], WD_PAD))
    quant_jobs.reverse()

    def emit_quant(n):
        for _ in range(n):
            if quant_jobs:
                quant_tile(*quant_jobs.pop())

    def emit_ag1():
        emit_quant(len(quant_jobs))
        if DEBUG:
            OPS("gpsimd", lambda e: e.dma_start(out=dbg["d_wgq"], in_=wag_g_in),
                reads=[("wag_w",)], writes=[("dbgwg",)], is_dma=True)
        OPS("gpsimd", lambda e: e.collective_compute(
            "AllGather", ALU.bypass, replica_groups=[list(range(RNK))],
            ins=[wag_g_in], outs=[wag_g_out]),
            reads=[("wag_w",), ("qf", 15, 1)], writes=[("wag_out",)],
            stream=("cc", "1a"))
        OPS("gpsimd", lambda e: e.collective_compute(
            "AllGather", ALU.bypass, replica_groups=[list(range(RNK))],
            ins=[wag_d_in], outs=[wag_d_out]),
            reads=[("wag_w",), ("qf", 15, 1)], writes=[("wag_outd",)],
            stream=("cc", "1b"))

    def load_wslice(wdram, ic, oq, par, group=None):
        OPS("gpsimd", lambda e, ic=ic, oq=oq, par=par, wdram=wdram: e.dma_start(
            out=wbuf[par], in_=wdram[ic * P:(ic + 1) * P,
                                     oq * 512:(oq + 1) * 512]),
            writes=[("wbuf", par)], is_dma=True, group=group)

    for oq in range(4):
        for blk in range(0, NI, 4):
            for ic in range(blk, blk + 4):
                load_wslice(wv_d, ic, oq, ic % 4, group=("vw", oq, blk))
            for ic in range(blk, blk + 4):
                par = ic % 4
                first = ic == 0
                for tt in range(NT):
                    OPS("tensor", lambda e, ic=ic, tt=tt, par=par: e.matmul(
                        bank(tt), x_fm[ic][:, tt * P:(tt + 1) * P], wbuf[par],
                        start=(ic == 0), stop=False),
                        reads=[rA(ic), ("wbuf", par)],
                        acquire=psB(tt) if (first and tt >= 0) else ())
        for tt in range(NT):
            OPS("tensor", lambda e, oq=oq, tt=tt: e.matmul(
                bank(tt), ones_row[:], brow[:, oq * 512:(oq + 1) * 512],
                start=False, stop=True),
                reads=[("ones_r",), ("brow",)], commit=psB(tt))
        for tt in range(NT):
            OPS("scalar", lambda e, oq=oq, tt=tt: e.copy(
                v_tm[tt][:, oq * 512:(oq + 1) * 512], bank(tt)),
                reads=psB(tt), writes=[rD(tt), ("vtm", tt, oq)])
        emit_quant(15)

    def proj_fm(wdram, pb, nb, pres, nres, dst_fm, dblk, dres, qn=0,
                hook=None):
        for rnd in range(4):
            for blk in range(0, NI, 4):
                for ic in range(blk, blk + 4):
                    load_wslice(wdram, ic, rnd, ic % 4, group=(id(wdram), rnd, blk))
                for ic in range(blk, blk + 4):
                    par = ic % 4
                    for j in range(4):
                        for half in range(2):
                            b = 2 * j + half
                            first = ic == 0; last = ic == NI - 1
                            OPS("tensor",
                                lambda e, ic=ic, j=j, half=half, par=par, b=b: e.matmul(
                                    bank(b), wbuf[par][:, j * P:(j + 1) * P],
                                    x_fm[ic][:, half * 512:(half + 1) * 512],
                                    start=(ic == 0), stop=(ic == NI - 1)),
                                reads=[rA(ic), ("wbuf", par)],
                                acquire=psB(b) if first else (),
                                commit=psB(b) if last else ())
            if rnd == 0 and hook is not None:
                hook()
            for j in range(4):
                oc = rnd * 4 + j
                for half in range(2):
                    b = 2 * j + half
                    OPS("scalar", lambda e, b=b, oc=oc: e.activation(
                        t1[:], bank(b), AF.Relu, bias=sc(nb + oc), scale=-1.0),
                        reads=psB(b) + [nres], writes=[("t1",)])
                    OPS("scalar", lambda e: e.activation(
                        t2[:], t1[:], AF.Exp, scale=-1.0),
                        reads=[("t1",)], writes=[("t2",)])
                    OPS("scalar", lambda e, b=b, oc=oc: e.activation(
                        t3[:], bank(b), AF.Relu, bias=sc(pb + oc), scale=1.0),
                        reads=psB(b) + [pres], writes=[("t3",)])
                    OPS("vector",
                        lambda e, oc=oc, half=half: e.tensor_add(
                            dst_fm[oc][:, half * 512:(half + 1) * 512],
                            t2[:], t3[:]),
                        reads=[("t2",), ("t3",)],
                        writes=[dblk(oc), (dres, oc, half)])
            emit_quant(qn[rnd] if isinstance(qn, (list, tuple)) else qn)

    proj_fm(wk_d, BK, NBK, ("c_bk",), ("c_nbk",), kf_fm, rCf, "kf", qn=[14, 14, 9, 7])

    for oc in range(NI):
        for tt in range(NT):
            s8 = (oc * NT + tt) % 8
            pcol = s8 * 1024
            pres = psq(s8, 0)
            OPS("tensor", lambda e, oc=oc, tt=tt, pcol=pcol: e.transpose(
                PS_bf[:, pcol:pcol + P],
                kf_fm[oc][:, tt * P:(tt + 1) * P], ident_bf[:]),
                reads=[("kf", oc, tt // 4), ("ident",)],
                writes=[pres])
            OPS("scalar", lambda e, oc=oc, tt=tt, pcol=pcol: e.copy(
                kf_tm[tt][:, oc * P:(oc + 1) * P],
                PS_bf[:, pcol:pcol + P]),
                reads=[pres], writes=[rCt(tt), ("kt", tt, oc)])

    for h in range(H):
        b = h // 4
        col = (h % 4) * P
        for n in range(NT):
            first = n == 0 and h % 4 == 0
            last = n == NT - 1 and h % 4 == 3
            OPS("tensor", lambda e, h=h, n=n, b=b, col=col: e.matmul(
                PS[:, b * 512 + col:b * 512 + col + P],
                kf_tm[n][:, h * P:(h + 1) * P], v_tm[n][:, h * P:(h + 1) * P],
                start=(n == 0), stop=(n == NT - 1)),
                reads=[("kt", n, h), ("vtm", n, h // 4), rCt(n), rD(n)],
                acquire=[psq(b, h % 4)] if n == 0 else (),
                commit=[psq(b, h % 4)] if n == NT - 1 else ())
    for h in range(H):
        for n in range(NT):
            first = n == 0 and h == 0
            last = n == NT - 1 and h == H - 1
            OPS("tensor", lambda e, h=h, n=n: e.matmul(
                PS[:, 4 * 512 + h:4 * 512 + h + 1],
                kf_tm[n][:, h * P:(h + 1) * P], ones_col[:],
                start=(n == 0), stop=(n == NT - 1)),
                reads=[("kt", n, h), ("ones_c",), rCt(n)],
                acquire=[psq(4, 0)] if first else (),
                commit=[psq(4, 0)] if last else ())
    for b4 in range(4):
        tb = t1 if b4 % 2 == 0 else t2
        rt = ("t1",) if b4 % 2 == 0 else ("t2",)
        OPS("scalar", lambda e, b4=b4, tb=tb: e.copy(tb[:], bank(b4)),
            reads=psB(b4), writes=[rt])
        OPS("sync", lambda e, b4=b4, tb=tb: e.dma_start(
            out=kag_in[:, b4 * 512:(b4 + 1) * 512], in_=tb[:]),
            reads=[rt], writes=[("kag_w",)], is_dma=True)
    OPS("scalar", lambda e: e.copy(t3[:, 0:H], PS[:, 4 * 512:4 * 512 + H]),
        reads=[psq(4, 0)], writes=[("t3",)])
    OPS("sync", lambda e: e.dma_start(out=kag_in[:, DM:DM + H], in_=t3[:, 0:H]),
        reads=[("t3",)], writes=[("kag_w",)], is_dma=True)
    if DEBUG:
        OPS("sync", lambda e: e.dma_start(out=dbg["d_kvtot"], in_=kag_in),
            reads=[("kag_w",)], writes=[("dbg_kv",)], is_dma=True)
    def ag2_prefix_hook():
        OPS("gpsimd", lambda e: e.collective_compute(
            "AllGather", ALU.bypass, replica_groups=[list(range(RNK))],
            ins=[kag_in], outs=[kag_out]),
            reads=[("kag_w",)], writes=[("kag_out",)], stream=("cc", "2"))
        for r in range(RNK):
            if r % 2 == 0:
                st_kv = regX[:, 0:2048]
                r_kv = rXb(0, 2048)
                st_ks = regX[:, 2048:2048 + H]
                r_ks = rXb(2048, 2048 + H)
            else:
                st_kv = regB_f32q[:, 6144:8192]
                r_kv = [("rB", bb) for bb in range(12, 16)]
                st_ks = regX[:, 2064:2064 + H]
                r_ks = rXb(2064, 2064 + H)
            OPS("scalar", lambda e, r=r, st_kv=st_kv: e.dma_start(
                out=st_kv, in_=kag_out[r * P:(r + 1) * P, 0:DM]),
                reads=[("kag_out",)], writes=r_kv, is_dma=True)
            OPS("scalar", lambda e, r=r, st_ks=st_ks: e.dma_start(
                out=st_ks, in_=kag_out[r * P:(r + 1) * P, DM:DM + H]),
                reads=[("kag_out",)], writes=r_ks, is_dma=True)
            OPS("vector", lambda e, r=r, st_kv=st_kv: e.scalar_tensor_tensor(
                Skv[:], st_kv, pm_sb[:, r:r + 1], Skv[:], ALU.mult, ALU.add),
                reads=r_kv + [("pm",), ("Skv",)], writes=[("Skv",)])
            OPS("vector", lambda e, r=r, st_ks=st_ks: e.scalar_tensor_tensor(
                Sks[:], st_ks, pm_sb[:, r:r + 1], Sks[:], ALU.mult, ALU.add),
                reads=r_ks + [("pm",), ("Sks",)], writes=[("Sks",)])
        OPS("vector", lambda e: e.tensor_copy(Skv_bf[:], Skv[:]),
            reads=[("Skv",)], writes=[("Skvb",)])
        OPS("vector", lambda e: e.tensor_copy(Sks_bf[:], Sks[:]),
            reads=[("Sks",)], writes=[("Sksb",)])

    proj_fm(wq_d, BQ, NBQ, ("c_bq",), ("c_nbq",), qf_fm, rB, "qf",
            hook=ag2_prefix_hook)

    if DEBUG:
        for ic in range(NI):
            OPS("gpsimd", lambda e, ic=ic: e.dma_start(
                out=dbg["d_qf"][ic * P:(ic + 1) * P, :], in_=qf_fm[ic]),
                reads=[("qf", ic, 0), ("qf", ic, 1), rB(ic)], writes=[("dbgq",)],
                is_dma=True)
            OPS("gpsimd", lambda e, ic=ic: e.dma_start(
                out=dbg["d_kf"][ic * P:(ic + 1) * P, :], in_=kf_fm[ic]),
                reads=[("kf", ic, 0), ("kf", ic, 1), rCf(ic)], writes=[("dbgk",)],
                is_dma=True)
        for tt in range(NT):
            OPS("gpsimd", lambda e, tt=tt: e.dma_start(
                out=dbg["d_v"][tt * P:(tt + 1) * P, :], in_=v_tm[tt]),
                reads=[("vtm", tt, q) for q in range(4)] + [rD(tt)], writes=[("dbgv",)],
                is_dma=True)

    emit_ag1()

    for n in range(NT):
        for h in range(H):
            b = h // 4
            col = (h % 4) * P
            OPS("tensor", lambda e, h=h, n=n, b=b, col=col: e.matmul(
                PS[:, b * 512 + col:b * 512 + col + P],
                kf_tm[n][:, h * P:(h + 1) * P], v_tm[n][:, h * P:(h + 1) * P],
                start=True, stop=True),
                reads=[("kt", n, h), ("vtm", n, h // 4), rCt(n), rD(n)],
                writes=[psq(b, h % 4)])
        for h in range(H):
            OPS("tensor", lambda e, h=h, n=n: e.matmul(
                PS[:, 4 * 512 + h:4 * 512 + h + 1],
                kf_tm[n][:, h * P:(h + 1) * P], ones_col[:],
                start=True, stop=True),
                reads=[("kt", n, h), ("ones_c",), rCt(n)],
                acquire=[psq(4, 0)] if h == 0 else (),
                commit=[psq(4, 0)] if h == H - 1 else ())
        def pe_A(h):
            par = h % 2
            OPS("tensor", lambda e, h=h, n=n, par=par: e.matmul(
                PS[:, 5 * 512 + par * P:5 * 512 + (par + 1) * P],
                kf_fm[h][:, n * P:(n + 1) * P], qf_fm[h][:, n * P:(n + 1) * P],
                start=True, stop=True),
                reads=[("kf", h, n // 4), ("qf", h, n // 4), rCf(h), rB(h)],
                writes=[psq(5, par)])

        def dve_mask(h):
            par = h % 2
            OPS("vector", lambda e, par=par: e.tensor_mul(
                Asb[par], PS[:, 5 * 512 + par * P:5 * 512 + (par + 1) * P],
                mask_f[:]),
                reads=[psq(5, par), ("mask",)], writes=[("Asb", par)])

        def pe_I(h):
            par = h % 2
            OPS("tensor", lambda e, h=h, n=n, par=par: e.matmul(
                PS[:, 6 * 512 + par * P:6 * 512 + (par + 1) * P],
                Asb[par], v_tm[n][:, h * P:(h + 1) * P],
                start=True, stop=False),
                reads=[("Asb", par), ("vtm", n, h // 4), rD(n)],
                acquire=[psq(6, par)])
            OPS("tensor", lambda e, h=h, n=n, par=par: e.matmul(
                PS[:, 6 * 512 + par * P:6 * 512 + (par + 1) * P],
                qf_fm[h][:, n * P:(n + 1) * P], Skv_bf[:, h * P:(h + 1) * P],
                start=False, stop=True),
                reads=[("qf", h, n // 4), ("Skvb",), rB(h)],
                commit=[psq(6, par)])
            OPS("tensor", lambda e, h=h, n=n: e.matmul(
                PS[:, 4 * 512 + 16 + h:4 * 512 + 17 + h],
                Asb[h % 2], ones_col[:], start=True, stop=False),
                reads=[("Asb", h % 2), ("ones_c",)], acquire=[("ps4z", h)])
            OPS("tensor", lambda e, h=h, n=n: e.matmul(
                PS[:, 4 * 512 + 16 + h:4 * 512 + 17 + h],
                qf_fm[h][:, n * P:(n + 1) * P], Sks_bf[:, h:h + 1],
                start=False, stop=True),
                reads=[("qf", h, n // 4), ("Sksb",), rB(h)],
                commit=[("ps4z", h)])

        def act_zeps(h):
            OPS("scalar", lambda e, h=h: e.activation(
                sc(ZEPS + h), PS[:, 4 * 512 + 16 + h:4 * 512 + 17 + h],
                AF.Identity, bias=scl[:, 123:124]),
                reads=[("ps4z", h), ("c_eps",)], writes=[("zeps", h)])

        def dve_div(h):
            par = h % 2
            OPS("vector", lambda e, h=h: e.reciprocal(sc(ZR + h), sc(ZEPS + h)),
                reads=[("zeps", h)], writes=[("zr", h)])
            OPS("vector", lambda e, h=h, par=par: e.tensor_scalar(
                attm[par], PS[:, 6 * 512 + par * P:6 * 512 + (par + 1) * P],
                sc(ZR + h), None, ALU.mult),
                reads=[psq(6, par), ("zr", h)], writes=[("attm", par)])

        def pe_T(h):
            par = h % 2
            OPS("tensor", lambda e, par=par: e.transpose(
                PS_bf[:, 7 * 1024 + par * 256:7 * 1024 + par * 256 + P],
                attm[par], ident_bf[:]),
                reads=[("attm", par), ("ident",)], writes=[psq(7, par)])

        def act_evacA(h):
            par = h % 2
            OPS("scalar", lambda e, h=h, n=n, par=par: e.copy(
                att_fm[h][:, n * P:(n + 1) * P],
                PS_bf[:, 7 * 1024 + par * 256:7 * 1024 + par * 256 + P]),
                reads=[psq(7, par)], writes=[rA(h), ("af", h, n)])

        pe_A(0); dve_mask(0)
        pe_A(1); dve_mask(1)
        for h in range(H):
            pe_I(h)
            act_zeps(h)
            if h + 2 < H:
                pe_A(h + 2)
                dve_mask(h + 2)
            dve_div(h)
            pe_T(h)
            act_evacA(h)
        for b4 in range(4):
            OPS("vector", lambda e, b4=b4: e.tensor_add(
                Skv[:, b4 * 512:(b4 + 1) * 512],
                Skv[:, b4 * 512:(b4 + 1) * 512], bank(b4)),
                reads=psB(b4) + [("Skv",)], writes=[("Skv",)])
        OPS("vector", lambda e: e.tensor_add(
            Sks[:], Sks[:], PS[:, 4 * 512:4 * 512 + H]),
            reads=[psq(4, 0), ("Sks",)], writes=[("Sks",)])
        OPS("vector", lambda e: e.tensor_copy(Skv_bf[:], Skv[:]),
            reads=[("Skv",)], writes=[("Skvb",)])
        OPS("vector", lambda e: e.tensor_copy(Sks_bf[:], Sks[:]),
            reads=[("Sks",)], writes=[("Sksb",)])

    if DEBUG:
        for ic in range(NI):
            OPS("gpsimd", lambda e, ic=ic: e.dma_start(
                out=dbg["d_att"][ic * P:(ic + 1) * P, :], in_=att_fm[ic]),
                reads=[("af", ic, n2) for n2 in range(NT)] + [rA(ic)],
                writes=[("dbga",)], is_dma=True)

    OPS("gpsimd", lambda e: e.dma_start(
        out=brow[:], in_=bo_d.unsqueeze(0)),
        writes=[("brow",)], is_dma=True)
    xs_stage = [regX[:, 0:512], regX[:, 512:1024]]
    for oq in range(4):
        for blk in range(0, NI, 4):
            for ic in range(blk, blk + 4):
                load_wslice(wo_d, ic, oq, ic % 4, group=("wow", oq, blk))
            for ic in range(blk, blk + 4):
                par = ic % 4
                first = ic == 0
                for tt in range(NT):
                    OPS("tensor", lambda e, ic=ic, tt=tt, par=par: e.matmul(
                        bank(tt), att_fm[ic][:, tt * P:(tt + 1) * P], wbuf[par],
                        start=(ic == 0), stop=False),
                        reads=[("af", ic, tt), ("wbuf", par), rA(ic)],
                        acquire=psB(tt) if first else ())
        for tt in range(NT):
            OPS("tensor", lambda e, oq=oq, tt=tt: e.matmul(
                bank(tt), ones_row[:], brow[:, oq * 512:(oq + 1) * 512],
                start=False, stop=True),
                reads=[("ones_r",), ("brow",)], commit=psB(tt))
        for tt in range(NT):
            par = tt % 2
            OPS("sync", lambda e, oq=oq, tt=tt, par=par: e.dma_start(
                out=xs_stage[par],
                in_=x_sl[tt * P:(tt + 1) * P, oq * 512:(oq + 1) * 512]),
                writes=rXb(par * 512, par * 512 + 512), is_dma=True)
            OPS("vector", lambda e, oq=oq, tt=tt, par=par: e.tensor_add(
                x1n[tt][:, oq * 512:(oq + 1) * 512], xs_stage[par], bank(tt)),
                reads=rXb(par * 512, par * 512 + 512) + psB(tt),
                writes=[rC_x1(tt, oq), ("x1", tt, oq)])

    for tt in range(NT):
        rx1 = [("x1", tt, q) for q in range(4)]
        for q4 in range(4):
            OPS("vector", lambda e, tt=tt, q4=q4: e.bn_stats(
                bnst[:, q4, :], x1n[tt][:, q4 * 512:(q4 + 1) * 512]),
                reads=[("x1", tt, q4)], writes=[("bnst", q4)])
        OPS("vector", lambda e: e.bn_aggr(
            sc(BNA, 2), bnst.rearrange("p a b -> p (a b)")),
            reads=[("bnst", q) for q in range(4)], writes=[("c_bna",)])
        OPS("vector", lambda e: e.tensor_scalar(
            sc(LNV), sc(BNA + 1), EPS_LN, None, ALU.add),
            reads=[("c_bna",)], writes=[("c_lnv",)])
        OPS("scalar", lambda e: e.activation(sc(LNV), sc(LNV), AF.Ln),
            reads=[("c_lnv",)], writes=[("c_lnv",)])
        OPS("scalar", lambda e: e.activation(
            sc(LNR), sc(LNV), AF.Exp, scale=-0.5),
            reads=[("c_lnv",)], writes=[("c_lnr",)])
        OPS("vector", lambda e: e.scalar_tensor_tensor(
            sc(LNB), sc(BNA), -1.0, sc(LNR), ALU.mult, ALU.mult),
            reads=[("c_bna",), ("c_lnr",)], writes=[("c_lnb",)])
        OPS("scalar", lambda e, tt=tt: e.activation(
            x1n[tt], x1n[tt], AF.Identity, bias=sc(LNB), scale=sc(LNR)),
            reads=rx1 + [("c_lnb",), ("c_lnr",)],
            writes=[("x1", tt, q) for q in range(4)] +
                   [rC_x1(tt, q) for q in range(4)])
        OPS("vector", lambda e, tt=tt: e.tensor_reduce(
            sc(AMX), x1n[tt], AX, ALU.max, apply_absolute_value=True),
            reads=rx1, writes=[("c_amx",)])
        OPS("vector", lambda e: e.tensor_scalar(
            sc(AMX), sc(AMX), EPS_FFN, None, ALU.max),
            reads=[("c_amx",)], writes=[("c_amx",)])
        OPS("vector", lambda e: e.reciprocal(sc(SFC), sc(AMX)),
            reads=[("c_amx",)], writes=[("c_sfc",)])
        OPS("vector", lambda e: e.tensor_scalar(
            sc(SFC), sc(SFC), 127.0, None, ALU.mult),
            reads=[("c_sfc",)], writes=[("c_sfc",)])
        OPS("vector", lambda e: e.reciprocal(sc(ISF), sc(SFC)),
            reads=[("c_sfc",)], writes=[("c_isf",)])
        for hh in range(2):
            sl = slice(hh * 1024, (hh + 1) * 1024)
            ti = regX_i32[:, 0:1024]
            e_ = regX[:, 1024:2048]
            f_ = regX[:, 2048:3072]
            OPS("vector", lambda e, tt=tt, sl=sl, ti=ti: e.tensor_scalar(
                ti, x1n[tt][:, sl], sc(SFC), None, ALU.mult),
                reads=rx1 + [("c_sfc",)], writes=rXb(0, 1024))
            OPS("vector", lambda e, tt=tt, sl=sl, ti=ti, e_=e_:
                e.scalar_tensor_tensor(
                    e_, x1n[tt][:, sl], sc(SFC), ti, ALU.mult, ALU.subtract),
                reads=rx1 + [("c_sfc",)] + rXb(0, 1024),
                writes=rXb(1024, 2048))
            OPS("vector", lambda e, e_=e_, f_=f_: e.tensor_scalar(
                f_, e_, -0.5, None, ALU.is_lt),
                reads=rXb(1024, 2048), writes=rXb(2048, 3072))
            OPS("vector", lambda e, ti=ti, e_=e_: e.scalar_tensor_tensor(
                e_, e_, 0.5, ti, ALU.is_gt, ALU.add),
                reads=rXb(1024, 2048) + rXb(0, 1024), writes=rXb(1024, 2048))
            OPS("vector", lambda e, e_=e_, f_=f_: e.tensor_sub(
                e_, e_, f_),
                reads=rXb(1024, 2048) + rXb(2048, 3072),
                writes=rXb(1024, 2048))
            OPS("scalar", lambda e, tt=tt, sl=sl, e_=e_: e.mul(
                xq_tm[tt][:, sl], e_, sc(ISF)),
                reads=rXb(1024, 2048) + [("c_isf",)],
                writes=[rD(tt), ("xqt", tt, sl.start)])
        OPS("sync", lambda e, tt=tt: e.dma_start(
            out=xq_dr[tt * P:(tt + 1) * P, :], in_=xq_tm[tt]),
            reads=[("xqt", tt, 0), ("xqt", tt, 1024), rD(tt)],
            writes=[("xqd", tt)], is_dma=True)
        if DEBUG:
            OPS("sync", lambda e, tt=tt: e.dma_start(
                out=dbg["d_x1n"][tt * P:(tt + 1) * P, :], in_=x1n[tt]),
                reads=[("x1", tt, q) for q in range(4)] +
                      [rC_x1(tt, q) for q in range(4)],
                writes=[("dbgx1",)], is_dma=True)
            OPS("gpsimd", lambda e, tt=tt: e.dma_start(
                out=dbg["d_xq"][tt * P:(tt + 1) * P, :], in_=xq_tm[tt]),
                reads=[("xqt", tt, 0), ("xqt", tt, 1024), rD(tt)], writes=[("dbgxq",)], is_dma=True)

    for ic in range(NI):
        OPS("sync", lambda e, ic=ic: e.dma_start_transpose(
            xq_fm[ic], xq_dr[:, ic * P:(ic + 1) * P]),
            reads=[("xqd", tt2) for tt2 in range(NT)],
            writes=[rB(ic), ("xqf", ic)], is_dma=True, group=("xqfm",))

    for os in range(len(OS_W)):
        w = OS_W[os]; o0 = OS_OFF[os]
        par = os % 2
        gres, ures = [], []
        for ic in range(NI):
            r = ic // 2
            res_g = ("f1wg", par, ic)
            OPS("sync", lambda e, ic=ic, r=r, o0=o0, w=w, par=par:
                e.dma_start_transpose(
                    wgbuf(par, ic)[:, 0:w],
                    wag_g_out[r * GU_ROWS + o0:r * GU_ROWS + o0 + w,
                              (ic % 2) * P:(ic % 2 + 1) * P]),
                reads=[("wag_out",)], writes=[rWg(par, ic), res_g],
                is_dma=True, group=("f1w", os))
            gres.append(res_g)
            res_u = ("f1wu", par, ic)
            OPS("sync", lambda e, ic=ic, r=r, o0=o0, w=w, par=par:
                e.dma_start_transpose(
                    wubuf(par, ic)[:, 0:w],
                    wag_g_out[r * GU_ROWS + DFF + o0:r * GU_ROWS + DFF + o0 + w,
                              (ic % 2) * P:(ic % 2 + 1) * P]),
                reads=[("wag_out",)], writes=[rWu(par, ic), res_u],
                is_dma=True, group=("f1w", os))
            ures.append(res_u)
        for half in range(2):
            tts = range(half * 4, half * 4 + 4)
            for ic in range(NI):
                for tt in tts:
                    bg = tt % 4; bu = 4 + tt % 4
                    first = ic == 0; last = ic == NI - 1
                    OPS("tensor",
                        lambda e, ic=ic, tt=tt, par=par, bg=bg, w=w: e.matmul(
                            bank(bg, w), xq_fm[ic][:, tt * P:(tt + 1) * P],
                            wgbuf(par, ic)[:, 0:w],
                            start=(ic == 0), stop=(ic == NI - 1)),
                        reads=[("xqf", ic), gres[ic], rWg(par, ic), rB(ic)],
                        acquire=psB(bg, w) if first else (),
                        commit=psB(bg, w) if last else ())
                    OPS("tensor",
                        lambda e, ic=ic, tt=tt, par=par, bu=bu, w=w: e.matmul(
                            bank(bu, w), xq_fm[ic][:, tt * P:(tt + 1) * P],
                            wubuf(par, ic)[:, 0:w],
                            start=(ic == 0), stop=(ic == NI - 1)),
                        reads=[("xqf", ic), ures[ic], rWu(par, ic), rB(ic)],
                        acquire=psB(bu, w) if first else (),
                        commit=psB(bu, w) if last else ())
            for tt in tts:
                bg = tt % 4; bu = 4 + tt % 4
                te, rte = (t1, ("t1",)) if tt % 2 == 0 else (t2, ("t2",))
                to, rto = (t3, ("t3",)) if tt % 2 == 0 else (t1, ("t1b",))
                if tt % 2 == 1:
                    rto = ("t1",)
                uc0 = 1376 + (tt % 2) * 512
                ucp = regX[:, uc0:uc0 + 512]
                rucp = [("f1u", tt % 2)] + rXb(uc0, uc0 + 512)
                OPS("scalar", lambda e, bg=bg, w=w, te=te: e.activation(
                    te[:, 0:w], bank(bg, w), AF.Silu),
                    reads=psB(bg, w), writes=[rte])
                OPS("scalar", lambda e, bu=bu, w=w, ucp=ucp: e.copy(
                    ucp[:, 0:w], bank(bu, w)),
                    reads=psB(bu, w), writes=rucp)
                OPS("vector", lambda e, w=w, te=te, to=to, ucp=ucp:
                    e.tensor_mul(to[:, 0:w], te[:, 0:w], ucp[:, 0:w]),
                    reads=[rte] + rucp, writes=[rto])
                OPS("vector", lambda e, w=w, to=to: e.tensor_reduce(
                    sc(AMXT), to[:, 0:w], AX, ALU.max,
                    apply_absolute_value=True),
                    reads=[rto], writes=[("c_amxt",)])
                OPS("vector", lambda e, tt=tt: e.tensor_max(
                    hamax[:, tt:tt + 1], hamax[:, tt:tt + 1], sc(AMXT)),
                    reads=[("c_amxt",), ("ham", tt)], writes=[("ham", tt)])
                OPS("scalar", lambda e, tt=tt, o0=o0, w=w, to=to: e.dma_start(
                    out=hid_dr[tt * P:(tt + 1) * P, o0:o0 + w], in_=to[:, 0:w]),
                    reads=[rto], writes=[("hidd", tt, os)], is_dma=True)

    regD_f32 = regD.bitcast(F32)
    f2_e_ = regD_f32[:, 3072:4448]
    r_f2e = [("rD", 3), ("rD", 4)]
    f2_hidq = [
        (regD_f32[:, 4448:5824], [("rD", 4), ("rD", 5)]),
        (regD_f32[:, 5824:7200], [("rD", 5), ("rD", 6), ("rD", 7)]),
        (regB_f32[:, 5120:6496], [("rB", 10), ("rB", 11), ("rB", 12)]),
    ]
    f2_hqq = [
        (regB[:, 12992:14368], [("rB", 12), ("rB", 13)]),
        (regB[:, 14368:15744], [("rB", 14), ("rB", 15)]),
    ]
    f2_ti = regX_i32[:, 0:1376]
    f2_f_ = regX[:, 1376:2752]
    HAM2S = [HAM2, 124]
    SF2S = [SF2, 125]
    IS2S = [IS2, 126]
    _f2j = [0]

    def f2_job(tt, q):
        j = _f2j[0]; _f2j[0] += 1
        hidq, rh = f2_hidq[j % 3]
        hqq, rq = f2_hqq[j % 2]
        sH, sS, sI = HAM2S[j % 2], SF2S[j % 2], IS2S[j % 2]
        q0 = q * 1376
        os_lo = q0 // 512
        os_hi = (q0 + 1375) // 512
        OPS("gpsimd", lambda e, tt=tt, q0=q0, hidq=hidq: e.dma_start(
            out=hidq, in_=hid_dr[tt * P:(tt + 1) * P, q0:q0 + 1376]),
            reads=[("hidd", tt, os2) for os2 in range(os_lo, os_hi + 1)],
            writes=rh, is_dma=True)
        OPS("vector", lambda e, tt=tt, sH=sH: e.tensor_scalar(
            sc(sH), hamax[:, tt:tt + 1], EPS_FFN, None, ALU.max),
            reads=[("ham", tt)], writes=[("c_ham2", sH)])
        OPS("vector", lambda e, sH=sH, sS=sS: e.reciprocal(sc(sS), sc(sH)),
            reads=[("c_ham2", sH)], writes=[("c_sf2", sS)])
        OPS("vector", lambda e, sS=sS: e.tensor_scalar(
            sc(sS), sc(sS), 127.0, None, ALU.mult),
            reads=[("c_sf2", sS)], writes=[("c_sf2", sS)])
        OPS("vector", lambda e, sS=sS, sI=sI: e.reciprocal(sc(sI), sc(sS)),
            reads=[("c_sf2", sS)], writes=[("c_is2", sI)])
        OPS("vector", lambda e, hidq=hidq, sS=sS: e.tensor_scalar(
            f2_ti, hidq, sc(sS), None, ALU.mult),
            reads=rh + [("c_sf2", sS)], writes=rXb(0, 1376))
        OPS("vector", lambda e, hidq=hidq, sS=sS: e.scalar_tensor_tensor(
            f2_e_, hidq, sc(sS), f2_ti, ALU.mult, ALU.subtract),
            reads=rh + rXb(0, 1376) + [("c_sf2", sS)], writes=r_f2e)
        OPS("vector", lambda e: e.tensor_scalar(
            f2_f_, f2_e_, -0.5, None, ALU.is_lt),
            reads=r_f2e, writes=rXb(1376, 2752))
        OPS("vector", lambda e: e.scalar_tensor_tensor(
            f2_e_, f2_e_, 0.5, f2_ti, ALU.is_gt, ALU.add),
            reads=r_f2e + rXb(0, 1376), writes=r_f2e)
        OPS("vector", lambda e: e.tensor_sub(f2_e_, f2_e_, f2_f_),
            reads=r_f2e + rXb(1376, 2752), writes=r_f2e)
        OPS("scalar", lambda e, hqq=hqq, sI=sI: e.mul(hqq, f2_e_, sc(sI)),
            reads=r_f2e + [("c_is2", sI)], writes=rq + [("f2hq", j % 2)])
        OPS("scalar", lambda e, tt=tt, q0=q0, hqq=hqq, j=j: e.dma_start(
            out=hq_dr[tt * P:(tt + 1) * P, q0:q0 + 1376], in_=hqq),
            reads=rq + [("f2hq", j % 2)],
            writes=[("hqd", tt), ("hqd2", tt, q)], is_dma=True)

    def emit_hqfm(half, qlevel):
        hrow0 = half * 512
        tts = list(range(half * 4, half * 4 + 4))
        for fc in range(NF):
            if (fc * P + P - 1) // 1376 != qlevel:
                continue
            OPS("sync", lambda e, fc=fc, hrow0=hrow0: e.dma_start_transpose(
                hq_fm(fc), hq_dr[hrow0:hrow0 + 512, fc * P:(fc + 1) * P]),
                reads=[("hqd2", tt2, qq) for tt2 in tts
                       for qq in range(qlevel + 1)],
                writes=[rHq(fc), ("hqf", fc)], is_dma=True,
                group=("hqfm", half, qlevel))

    def f3_half(half, post_pair=None):
        tts = list(range(half * 4, half * 4 + 4))
        emitted_lvl = [-1]

        def need_lvl(lvl):
            while emitted_lvl[0] < lvl:
                emitted_lvl[0] += 1
                emit_hqfm(half, emitted_lvl[0])

        for pair in range(2):
            for gidx, fb2 in enumerate(range(0, NF, 2)):
                fcs = [f for f in (fb2, fb2 + 1) if f < NF]
                if pair == 0:
                    look = min(fb2 + 3, NF - 1)
                    need_lvl((look * P + P - 1) // 1376)
                for fc in fcs:
                    for oq2 in range(2):
                        oq4 = pair * 2 + oq2
                        par = 2 * (fc - fb2) + oq2
                        rr = next(r for r in range(RNK)
                                  if WD_RANGES[r][0] <= fc < WD_RANGES[r][1])
                        lcol = (fc - WD_RANGES[rr][0]) * P
                        OPS("sync", lambda e, oq4=oq4, par=par, rr=rr,
                            lcol=lcol: e.dma_start_transpose(
                                wdbuf[par],
                                wag_d_out[rr * DM + oq4 * 512:
                                          rr * DM + (oq4 + 1) * 512,
                                          lcol:lcol + P]),
                            reads=[("wag_outd",)],
                            writes=[("wdb", par), ("rB", 8 + par // 2)],
                            is_dma=True, group=("wdw", half, pair, fb2))
                for fc in fcs:
                    for oq2 in range(2):
                        par = 2 * (fc - fb2) + oq2
                        for tt in tts:
                            b = (tt % 4) * 2 + oq2
                            OPS("tensor",
                                lambda e, fc=fc, tt=tt, par=par, b=b: e.matmul(
                                    bank(b),
                                    hq_fm(fc)[:, (tt % 4) * P:(tt % 4 + 1) * P],
                                    wdbuf[par],
                                    start=(fc == 0), stop=(fc == NF - 1)),
                                reads=[("hqf", fc), ("wdb", par),
                                       ("rB", 8 + par // 2), rHq(fc)],
                                acquire=psB(b) if fc == 0 else (),
                                commit=psB(b) if fc == NF - 1 else ())
            for tt in tts:
                for oq2 in range(2):
                    b = (tt % 4) * 2 + oq2
                    oq4 = pair * 2 + oq2
                    OPS("vector", lambda e, tt=tt, oq4=oq4, b=b: e.tensor_add(
                        x1n[tt][:, oq4 * 512:(oq4 + 1) * 512],
                        x1n[tt][:, oq4 * 512:(oq4 + 1) * 512], bank(b)),
                        reads=psB(b) + [("x1", tt, oq4)],
                        writes=[("x1", tt, oq4), rC_x1(tt, oq4)])
            if post_pair is not None:
                post_pair(pair)
        for tt in tts:
            rx2 = [("x1", tt, q) for q in range(4)]
            for q4 in range(4):
                OPS("vector", lambda e, tt=tt, q4=q4: e.bn_stats(
                    bnst[:, q4, :], x1n[tt][:, q4 * 512:(q4 + 1) * 512]),
                    reads=[("x1", tt, q4)], writes=[("bnst", q4)])
            OPS("vector", lambda e: e.bn_aggr(
                sc(BNA, 2), bnst.rearrange("p a b -> p (a b)")),
                reads=[("bnst", q) for q in range(4)], writes=[("c_bna",)])
            OPS("vector", lambda e: e.tensor_scalar(
                sc(LNV), sc(BNA + 1), EPS_LN, None, ALU.add),
                reads=[("c_bna",)], writes=[("c_lnv",)])
            OPS("scalar", lambda e: e.activation(sc(LNV), sc(LNV), AF.Ln),
                reads=[("c_lnv",)], writes=[("c_lnv",)])
            OPS("scalar", lambda e: e.activation(
                sc(LNR), sc(LNV), AF.Exp, scale=-0.5),
                reads=[("c_lnv",)], writes=[("c_lnr",)])
            OPS("vector", lambda e: e.scalar_tensor_tensor(
                sc(LNB), sc(BNA), -1.0, sc(LNR), ALU.mult, ALU.mult),
                reads=[("c_bna",), ("c_lnr",)], writes=[("c_lnb",)])
            par = tt % 2
            ot = out_tile[par]
            OPS("scalar", lambda e, tt=tt, ot=ot: e.activation(
                ot, x1n[tt], AF.Identity, bias=sc(LNB), scale=sc(LNR)),
                reads=rx2 + [("c_lnb",), ("c_lnr",)],
                writes=rOut(par) + [("outt", par)])
            OPS("scalar", lambda e, tt=tt, ot=ot: e.dma_start(
                out=out_sl[tt * P:(tt + 1) * P, :], in_=ot),
                reads=[("outt", par)], writes=[("outd", tt)], is_dma=True)

    if DEBUG:
        for tt in range(NT):
            OPS("gpsimd", lambda e, tt=tt: e.dma_start(
                out=dbg["d_hid"][tt * P:(tt + 1) * P, :],
                in_=hid_dr[tt * P:(tt + 1) * P, :]),
                reads=[("hidd", tt, os2) for os2 in range(len(OS_W))],
                writes=[("dbghid",)], is_dma=True)
    for q in range(4):
        for tt in range(4):
            f2_job(tt, q)

    def post0(pair):
        lvls = (0, 1) if pair == 0 else (2, 3)
        for lvl in lvls:
            for tt in range(4, 8):
                f2_job(tt, lvl)

    f3_half(0, post0)
    if DEBUG:
        for tt in range(NT):
            OPS("gpsimd", lambda e, tt=tt: e.dma_start(
                out=dbg["d_hq"][tt * P:(tt + 1) * P, :],
                in_=hq_dr[tt * P:(tt + 1) * P, :]),
                reads=[("hqd", tt)], writes=[("dbghq",)], is_dma=True)
    f3_half(1)

    OPS("gpsimd", lambda e: e.engine_nop(),
        reads=[("outd", tt) for tt in range(NT)] +
              ([("dbgq",), ("dbgk",), ("dbgv",), ("dbga",), ("dbgx1",),
                ("dbgxq",), ("dbg_kv",), ("dbgwg",), ("dbghid",), ("dbghq",)]
               if DEBUG else []))

    B.finalize(nc, ctx)
    with nc.Block() as block:
        B.replay(block)
    ctx.close()
    return nc


def _contig(a):
    return np.ascontiguousarray(np.asarray(a))


def kernel(L0_attn_x, L0_attn_Wq, L0_attn_bq, L0_attn_Wk, L0_attn_bk,
           L0_attn_Wv, L0_attn_bv, L0_attn_Wo, L0_attn_bo,
           L0_ffn_w_gate, L0_ffn_w_up, L0_ffn_w_down):
    x = np.asarray(L0_attn_x)
    wg = np.asarray(L0_ffn_w_gate); wu = np.asarray(L0_ffn_w_up)
    wd = np.asarray(L0_ffn_w_down)
    nc = build_nc()
    in_maps = []
    for c in range(RNK):
        b, p = c // 4, c % 4
        f0, f1 = WD_RANGES[c]
        wd_pad = np.zeros((DM, WD_PAD * P), np.float32)
        wd_pad[:, 0:(f1 - f0) * P] = wd[:, f0 * P:f1 * P]
        pm = np.zeros((P, RNK), np.float32)
        for r in range(RNK):
            if r // 4 == b and r % 4 < p:
                pm[:, r] = 1.0
        in_maps.append({
            "x_sl": _contig(x[b, p * T:(p + 1) * T]),
            "wq": _contig(L0_attn_Wq), "wk": _contig(L0_attn_Wk),
            "wv": _contig(L0_attn_Wv), "wo": _contig(L0_attn_Wo),
            "bq": _contig(L0_attn_bq), "bk": _contig(L0_attn_bk),
            "bv": _contig(L0_attn_bv), "bo": _contig(L0_attn_bo),
            "wg_sh": _contig(wg[:, 256 * c:256 * (c + 1)]),
            "wu_sh": _contig(wu[:, 256 * c:256 * (c + 1)]),
            "wd_sh": wd_pad,
            "pm": pm,
        })
    res = run_bass_kernel_spmd(nc, in_maps, core_ids=list(range(RNK)))
    out = np.zeros((2, 4096, DM), np.float32)
    for c in range(RNK):
        b, p = c // 4, c % 4
        out[b, p * T:(p + 1) * T] = res.results[c]["out_sl"]
    return out


if __name__ == "__main__":
    nc = build_nc()
    print("build ok, instructions:", nc.n_instructions())


# version 25
# speedup vs baseline: 1.0319x; 1.0288x over previous
import numpy as np
import ml_dtypes
import concourse.bass as bass
import concourse.mybir as mybir
from concourse.bass_utils import run_bass_kernel_spmd
from contextlib import ExitStack

F32 = mybir.dt.float32
BF16 = mybir.dt.bfloat16
I32 = mybir.dt.int32
AF = mybir.ActivationFunctionType
ALU = mybir.AluOpType
AX = mybir.AxisListType.X

P = 128
T = 1024
NT = 8
H = 16
DM = 2048
NI = DM // P
DFF = 5504
NF = DFF // P
EPS_ATT = 1e-6
EPS_LN = 1e-5
EPS_FFN = 1e-5
RNK = 8
K17 = float(2.0 ** 17)
K17I = float(2.0 ** -17)

DEBUG = False

WD_RANGES = [(NF * c // RNK, NF * (c + 1) // RNK) for c in range(RNK)]
WD_PAD = 6
GU_ROWS = 2 * DFF
OS_W = [512] * 10 + [384]
OS_OFF = [sum(OS_W[:i]) for i in range(len(OS_W))]


class Op:
    __slots__ = ("eng", "fn", "is_dma", "idx", "deps", "tagged", "tag_val",
                 "waits", "stream", "fence_wait", "group")
    def __init__(self, eng, fn, is_dma, idx, stream):
        self.eng, self.fn, self.is_dma, self.idx = eng, fn, is_dma, idx
        self.deps = set(); self.tagged = False; self.tag_val = None
        self.waits = {}; self.stream = stream; self.fence_wait = None
        self.group = None

class Builder:
    def __init__(self):
        self.ops = []
        self.res = {}

    def _st(self, r):
        if r not in self.res:
            self.res[r] = {"w": None, "r": set()}
        return self.res[r]

    def op(self, eng, fn, reads=(), writes=(), acquire=(), commit=(),
           is_dma=False, stream=None, group=None):
        idx = len(self.ops)
        stream = stream or (eng, "d" if is_dma else "c")
        o = Op(eng, fn, is_dma, idx, stream)
        o.group = group
        self.ops.append(o)
        for r in reads:
            st = self._st(r)
            if st["w"] is not None:
                o.deps.add(st["w"])
            st["r"].add(idx)
        for r in writes:
            st = self._st(r)
            if st["w"] is not None:
                o.deps.add(st["w"])
            o.deps.update(st["r"])
            st["r"] = set(); st["w"] = idx
        for r in acquire:
            st = self._st(r)
            if st["w"] is not None:
                o.deps.add(st["w"])
            o.deps.update(st["r"])
            st["r"] = set(); st["w"] = None
        for r in commit:
            st = self._st(r)
            st["w"] = idx; st["r"] = set()
        o.deps.discard(idx)
        return idx

    def finalize(self, nc, ctx):
        ops = self.ops
        heads = {}
        for o in ops:
            if o.group is None:
                continue
            key = (o.group, o.stream)
            if key not in heads:
                heads[key] = o
            else:
                h = heads[key]
                kept = set()
                for d in o.deps:
                    if d < h.idx:
                        h.deps.add(d)
                    elif d != h.idx:
                        kept.add(d)
                o.deps = kept
        by_eng = {}
        for o in ops:
            by_eng.setdefault(o.eng, []).append(o)

        def compute_bursts():
            burst_of = {}
            burst_members = {}
            nb = {}
            for eng, olist in by_eng.items():
                open_bursts = {}
                for o in olist:
                    if o.deps:
                        open_bursts.clear()
                    if o.is_dma:
                        s = o.stream
                        if s not in open_bursts:
                            nb[s] = nb.get(s, -1) + 1
                            open_bursts[s] = nb[s]
                        bid = (s, open_bursts[s])
                        burst_of[o.idx] = bid
                        burst_members.setdefault(bid, []).append(o.idx)
            return burst_of, burst_members

        for _round in range(40):
            burst_of, burst_members = compute_bursts()
            changed = False
            for o in ops:
                for d in list(o.deps):
                    do = ops[d]
                    if not do.is_dma:
                        continue
                    mem = burst_members[burst_of[d]]
                    if mem[-1] > o.idx:
                        k = next(i for i in mem if i > o.idx)
                        prev = mem[mem.index(k) - 1]
                        if prev not in ops[k].deps:
                            ops[k].deps.add(prev)
                            changed = True
            if not changed:
                break
        else:
            raise RuntimeError("fence injection did not converge")

        for o in ops:
            if o.is_dma:
                o.tagged = True
            for d in o.deps:
                ops[d].tagged = True
        tagcount = {}
        for o in ops:
            k = o.stream
            c = tagcount.get(k, 0)
            if o.tagged:
                c += 1
                o.tag_val = c * (16 if o.is_dma else 1)
            tagcount[k] = c
        burst_end_val = {}
        for bid, mem in burst_members.items():
            burst_end_val[bid] = max(ops[i].tag_val for i in mem)
        for o in ops:
            w = {}
            for d in o.deps:
                do = ops[d]
                if do.is_dma:
                    v = burst_end_val[burst_of[d]]
                else:
                    v = do.tag_val
                w[do.stream] = max(w.get(do.stream, 0), v)
            o.waits = w
        for bid, mem in burst_members.items():
            first = ops[mem[0]]
            s = first.stream
            prior = first.tag_val - 16
            if prior > 0:
                first.waits[s] = max(first.waits.get(s, 0), prior)
                first.fence_wait = (s, prior)
        self.sems = {}
        for k in sorted({o.stream for o in self.ops}, key=str):
            self.sems[k] = ctx.enter_context(nc.semaphore(f"s_{k[0]}_{k[1]}"))
        return tagcount

    def replay(self, block):
        by_eng = {}
        for o in self.ops:
            by_eng.setdefault(o.eng, []).append(o)
        hw = {}
        def mk(eng_name, ops):
            def body(eng):
                for o in ops:
                    wl = sorted(o.waits.items(), key=lambda kv: str(kv[0]))
                    for k, v in wl:
                        hk = (eng_name, k)
                        forced = o.fence_wait is not None and o.fence_wait[0] == k
                        if not forced and hw.get(hk, -1) >= v:
                            continue
                        eng.wait_ge(self.sems[k], v)
                        hw[hk] = max(hw.get(hk, -1), v)
                    inst = o.fn(eng)
                    if o.tagged:
                        inst.then_inc(self.sems[o.stream], 16 if o.is_dma else 1)
            return body
        for eng_name, ops in by_eng.items():
            getattr(block, eng_name)(mk(eng_name, ops))


def build_nc():
    nc = bass.Bass("TRN2", target_bir_lowering=False)
    B = Builder()
    ctx = ExitStack()
    E = ctx.enter_context
    OPS = B.op

    ap_in = lambda n, s: nc.dram_tensor(n, s, F32, kind="ExternalInput").ap()
    x_sl = ap_in("x_sl", [T, DM])
    wq_d = ap_in("wq", [DM, DM]); wk_d = ap_in("wk", [DM, DM])
    wv_d = ap_in("wv", [DM, DM]); wo_d = ap_in("wo", [DM, DM])
    bq_d = ap_in("bq", [DM]); bk_d = ap_in("bk", [DM])
    bv_d = ap_in("bv", [DM]); bo_d = ap_in("bo", [DM])
    wg_sh = ap_in("wg_sh", [DFF, 256])
    wu_sh = ap_in("wu_sh", [DFF, 256])
    wd_sh = ap_in("wd_sh", [DM, WD_PAD * P])
    pm_d = ap_in("pm", [P, RNK])
    out_sl = nc.dram_tensor("out_sl", [T, DM], F32, kind="ExternalOutput").ap()

    xbf_dr = nc.dram_tensor("xbf_dr", [T, DM], BF16, kind="Internal").ap()
    xq_dr = nc.dram_tensor("xq_dr", [T, DM], BF16, kind="Internal").ap()
    hid_dr = nc.dram_tensor("hid_dr", [T, DFF], F32, kind="Internal").ap()
    hq_dr = nc.dram_tensor("hq_dr", [T, DFF], BF16, kind="Internal").ap()
    wag_g_in = nc.dram_tensor("wag_g_in", [GU_ROWS, 256], BF16, kind="Internal").ap()
    wag_g_out = nc.dram_tensor("wag_g_out", [RNK * GU_ROWS, 256], BF16, kind="Internal").ap()
    wag_d_in = nc.dram_tensor("wag_d_in", [DM, WD_PAD * P], BF16, kind="Internal").ap()
    wag_d_out = nc.dram_tensor("wag_d_out", [RNK * DM, WD_PAD * P], BF16, kind="Internal").ap()
    kag_in = nc.dram_tensor("kag_in", [P, DM + H], F32, kind="Internal").ap()
    kag_out = nc.dram_tensor("kag_out", [RNK * P, DM + H], F32, kind="Internal").ap()

    dbg = {}
    if DEBUG:
        for nm, shape in (("d_qf", [DM, T]), ("d_kf", [DM, T]), ("d_v", [T, DM]),
                          ("d_kvtot", [P, DM + H]), ("d_att", [DM, T]),
                          ("d_x1n", [T, DM]), ("d_xq", [T, DM]),
                          ("d_wgq", [GU_ROWS, 256]), ("d_hid", [T, DFF]),
                          ("d_hq", [T, DFF])):
            dbg[nm] = nc.dram_tensor(nm, shape, F32, kind="ExternalOutput").ap()

    regA = E(nc.sbuf_tensor("regA", [P, 16 * 1024], BF16))
    regB = E(nc.sbuf_tensor("regB", [P, 16 * 1024], BF16))
    regC = E(nc.sbuf_tensor("regC", [P, 32 * 1024], BF16))
    regD = E(nc.sbuf_tensor("regD", [P, 16 * 1024], BF16))
    regX = E(nc.sbuf_tensor("regX", [P, 3712], F32))
    Skv = E(nc.sbuf_tensor("Skv", [P, DM], F32))
    Sks = E(nc.sbuf_tensor("Sks", [P, H], F32))
    Skv_bf = E(nc.sbuf_tensor("Skv_bf", [P, DM], BF16))
    Sks_bf = E(nc.sbuf_tensor("Sks_bf", [P, H], BF16))
    brow = E(nc.sbuf_tensor("brow", [1, DM], BF16))
    mask_f = E(nc.sbuf_tensor("mask_f", [P, P], F32))
    ident_bf = E(nc.sbuf_tensor("ident_bf", [P, P], BF16))
    ones_col = E(nc.sbuf_tensor("ones_col", [P, 1], BF16))
    ones_row = E(nc.sbuf_tensor("ones_row", [1, P], BF16))
    t1 = E(nc.sbuf_tensor("t1", [P, 512], F32))
    t2 = E(nc.sbuf_tensor("t2", [P, 512], F32))
    t3 = E(nc.sbuf_tensor("t3", [P, 512], F32))
    wstage = E(nc.sbuf_tensor("wstage", [P, 768], F32))
    wbuf_t = E(nc.sbuf_tensor("wbuf", [P, 4, 512], BF16))
    wbuf = [wbuf_t[:, i, :] for i in range(4)]
    wdbuf = None
    Asb_t = E(nc.sbuf_tensor("Asb", [P, 2, P], BF16))
    Asb = [Asb_t[:, 0, :], Asb_t[:, 1, :]]
    attm_t = E(nc.sbuf_tensor("attm", [P, 2, P], BF16))
    attm = [attm_t[:, 0, :], attm_t[:, 1, :]]
    scl = E(nc.sbuf_tensor("scl", [P, P], F32))
    hamax = E(nc.sbuf_tensor("hamax", [P, NT], F32))
    pm_sb = E(nc.sbuf_tensor("pm_sb", [P, RNK], F32))
    bnst = E(nc.sbuf_tensor("bnst", [P, 4, 6], F32))
    PS = E(nc.psum_tensor("PS", [P, 4096], F32))

    BQ, BK, NBQ, NBK = 0, 16, 32, 48
    ZEPS, ZR = 64, 80
    QSC, QRI = 96, 104
    AMX, SFC, ISF = 111, 112, 113
    BNA, LNV, LNR, LNB = 114, 116, 117, 118
    AMXT, HAM2, SF2, IS2 = 119, 120, 121, 122
    sc = lambda c, n=1: scl[:, c:c + n]

    x_fm = [regA[:, ic * 1024:(ic + 1) * 1024] for ic in range(NI)]
    att_fm = x_fm
    rA = lambda ic: ("rA", ic)
    qf_fm = [regB[:, ic * 1024:(ic + 1) * 1024] for ic in range(NI)]
    xq_fm = qf_fm
    rB = lambda ic: ("rB", ic)
    kf_fm = [regC[:, ic * 1024:(ic + 1) * 1024] for ic in range(NI)]
    kf_tm = [regC[:, 16 * 1024 + tt * 2048:16 * 1024 + (tt + 1) * 2048]
             for tt in range(NT)]
    rCf = lambda ic: ("rC", ic)
    rCt = lambda tt: ("rC", 16 + tt)
    v_tm = [regD[:, tt * 2048:(tt + 1) * 2048] for tt in range(NT)]
    rD = lambda tt: ("rD", tt)
    regC_f32 = regC.bitcast(F32)
    x1n = [regC_f32[:, tt * 2048:(tt + 1) * 2048] for tt in range(NT)]
    rX1 = lambda tt, q: ("rC", "x1", tt, q)
    def rC_x1(tt, q):
        bf_block = (tt * 4096 + q * 1024) // 1024
        if bf_block < 16:
            return ("rC", bf_block)
        return ("rC", 16 + (bf_block - 16) // 2)
    wgbuf = lambda par, ic: regA[:, (par * NI + ic) * 512:(par * NI + ic + 1) * 512]
    rWg = lambda par, ic: ("rA", (par * NI + ic) // 2)
    wubuf = lambda par, ic: regD[:, (par * NI + ic) * 512:(par * NI + ic + 1) * 512]
    rWu = lambda par, ic: ("rD", (par * NI + ic) // 4)
    def hq_fm(fc):
        if fc < 32:
            return regA[:, fc * 512:(fc + 1) * 512]
        return regD[:, (fc - 32) * 512:(fc - 31) * 512]
    def rHq(fc):
        if fc < 32:
            return ("rA", fc // 2)
        return ("rD", (fc - 32) // 4)
    xq_tm = [regD[:, tt * 2048:(tt + 1) * 2048] for tt in range(NT)]
    regB_f32 = regB.bitcast(F32)
    out_tile = [regB_f32[:, 0:2048], regB_f32[:, 2048:4096]]
    wdbuf = [regB[:, 8 * 1024 + i * 512:8 * 1024 + (i + 1) * 512]
             for i in range(4)]
    rOut = lambda par: [("rB", 4 * par), ("rB", 4 * par + 1),
                        ("rB", 4 * par + 2), ("rB", 4 * par + 3)]
    regX_i32 = regX.bitcast(I32)
    regX_bf = regX.bitcast(BF16)
    rXb = lambda c0, c1: [("rX", b) for b in range(c0 // 512, (c1 + 511) // 512)]
    xc_tmp = [regB[:, 12 * 1024:12 * 1024 + 2048],
              regB[:, 14 * 1024:14 * 1024 + 2048]]
    rXc = [[("rB", 12), ("rB", 13)], [("rB", 14), ("rB", 15)]]

    bank = lambda b, w=512: PS[:, b * 512:b * 512 + w]
    PS_bf = PS.bitcast(BF16)
    psq = lambda b, q: ("ps", b, q)
    psB = lambda b, w=512: [("ps", b, q) for q in range((w + 127) // 128)]

    OPS("gpsimd", lambda e: e.memset(mask_f[:], 1.0), writes=[("mask",)])
    OPS("gpsimd", lambda e: e.affine_select(
        out=mask_f[:], in_=mask_f[:], compare_op=ALU.is_ge, fill=0.0,
        base=0, pattern=[[1, P]], channel_multiplier=-1),
        reads=[("mask",)], writes=[("mask",)])
    OPS("gpsimd", lambda e: e.memset(ident_bf[:], 1.0), writes=[("ident",)])
    OPS("gpsimd", lambda e: e.affine_select(
        out=ident_bf[:], in_=ident_bf[:], compare_op=ALU.is_equal, fill=0.0,
        base=0, pattern=[[1, P]], channel_multiplier=-1),
        reads=[("ident",)], writes=[("ident",)])
    OPS("vector", lambda e: e.memset(ones_col[:], 1.0), writes=[("ones_c",)])
    OPS("vector", lambda e: e.memset(ones_row[:], 1.0), writes=[("ones_r",)])
    OPS("vector", lambda e: e.memset(Skv[:], 0.0), writes=[("Skv",)])
    OPS("vector", lambda e: e.memset(Sks[:], 0.0), writes=[("Sks",)])
    OPS("vector", lambda e: e.memset(hamax[:], 0.0),
        writes=[("ham", tt) for tt in range(NT)])
    OPS("vector", lambda e: e.memset(scl[:, 123:124], EPS_ATT),
        writes=[("c_eps",)])
    OPS("sync", lambda e: e.dma_start(
        out=sc(BQ, NI), in_=bq_d.rearrange("(o p) -> p o", p=P),
        allow_slow_non_contiguous=True),
        writes=[("c_bq",)], is_dma=True)
    OPS("sync", lambda e: e.dma_start(
        out=sc(BK, NI), in_=bk_d.rearrange("(o p) -> p o", p=P),
        allow_slow_non_contiguous=True),
        writes=[("c_bk",)], is_dma=True)
    OPS("sync", lambda e: e.dma_start(out=pm_sb[:], in_=pm_d),
        writes=[("pm",)], is_dma=True)
    OPS("gpsimd", lambda e: e.dma_start(
        out=brow[:], in_=bv_d.unsqueeze(0)),
        writes=[("brow",)], is_dma=True)
    OPS("vector", lambda e: e.tensor_scalar(
        sc(NBQ, NI), sc(BQ, NI), -1.0, None, ALU.mult),
        reads=[("c_bq",)], writes=[("c_nbq",)])
    OPS("vector", lambda e: e.tensor_scalar(
        sc(NBK, NI), sc(BK, NI), -1.0, None, ALU.mult),
        reads=[("c_bk",)], writes=[("c_nbk",)])

    for tt in range(NT):
        buf = xc_tmp[tt % 2]
        rr = rXc[tt % 2]
        OPS("gpsimd", lambda e, tt=tt, buf=buf: e.dma_start(
            out=buf, in_=x_sl[tt * P:(tt + 1) * P, :]),
            writes=rr, is_dma=True)
        OPS("gpsimd", lambda e, tt=tt, buf=buf: e.dma_start(
            out=xbf_dr[tt * P:(tt + 1) * P, :], in_=buf),
            reads=rr, writes=[("xbf", tt)], is_dma=True)
    for ic in range(NI):
        OPS("sync", lambda e, ic=ic: e.dma_start_transpose(
            x_fm[ic], xbf_dr[:, ic * P:(ic + 1) * P]),
            reads=[("xbf", tt2) for tt2 in range(NT)], writes=[rA(ic)],
            is_dma=True, group=("xfm",))

    regB_f32q = regB.bitcast(F32)
    _qpar = [0]

    def quant_tile(src_ap, dst_ap, G):
        W = G * P
        pp = _qpar[0] & 1
        _qpar[0] += 1
        if pp == 0:
            stage = wstage[:, 0:W]
            rstage = [("qstage", 0)]
            qbf = regX_bf[:, 6400:6400 + W]
            rqbf = rXb(3200, 3200 + (W + 1) // 2)
        else:
            stage = regB_f32q[:, 0:W]
            rstage = [("qstage", 1), ("rB", 0), ("rB", 1)]
            qbf = regB[:, 1536:1536 + W]
            rqbf = [("rB", 1), ("rB", 2)]
        OPS("scalar", lambda e: e.dma_start(out=stage, in_=src_ap),
            writes=rstage, is_dma=True)
        aw = regX[:, 0:W]
        OPS("scalar", lambda e: e.activation(aw, stage, AF.Abs),
            reads=rstage, writes=rXb(0, W))
        hi_i = regX_i32[:, 768:768 + W]
        OPS("scalar", lambda e: e.activation(hi_i, aw, AF.Identity, scale=K17),
            reads=rXb(0, W), writes=rXb(768, 768 + W))
        lo = regX[:, 1536:1536 + W]
        OPS("vector", lambda e: e.scalar_tensor_tensor(
            lo, hi_i, -K17I, aw, ALU.mult, ALU.add),
            reads=rXb(768, 768 + W) + rXb(0, W), writes=rXb(1536, 1536 + W))
        rh = regX[:, 2304:2304 + G]
        rl = regX[:, 2320:2320 + G]
        OPS("vector", lambda e: e.tensor_reduce(
            rh, hi_i.rearrange("p (g e) -> p g e", g=G), AX, ALU.add),
            reads=rXb(768, 768 + W), writes=rXb(2304, 2310))
        OPS("vector", lambda e: e.tensor_reduce(
            rl, lo.rearrange("p (g e) -> p g e", g=G), AX, ALU.add),
            reads=rXb(1536, 1536 + W), writes=rXb(2320, 2326))
        OPS("vector", lambda e: e.scalar_tensor_tensor(
            sc(QSC, G), rh, K17I, rl, ALU.mult, ALU.add),
            reads=rXb(2304, 2326), writes=[("c_qsc",)])
        OPS("vector", lambda e: e.tensor_scalar(
            sc(QSC, G), sc(QSC, G), 1.0 / P, EPS_FFN, ALU.mult, ALU.add),
            reads=[("c_qsc",)], writes=[("c_qsc",)])
        OPS("vector", lambda e: e.reciprocal(sc(QRI, G), sc(QSC, G)),
            reads=[("c_qsc",)], writes=[("c_qri",)])
        sgn = regX[:, 1536:1536 + W]
        OPS("scalar", lambda e: e.activation(sgn, stage, AF.Sign),
            reads=rstage + rXb(1536, 1536 + W),
            writes=rXb(1536, 1536 + W) + [("rX_sgn",)])
        m01 = regX[:, 2432:2432 + W]
        for g in range(G):
            OPS("vector", lambda e, g=g: e.tensor_scalar(
                m01[:, g * P:(g + 1) * P], aw[:, g * P:(g + 1) * P],
                sc(QRI + g), 0.5, ALU.mult, ALU.is_gt),
                reads=rXb(0, W) + [("c_qri",)],
                writes=[("rX_m", g)] + (rXb(2432, 2432 + W) if g == 0 else []))
        for g in range(G):
            OPS("vector", lambda e, g=g: e.scalar_tensor_tensor(
                qbf[:, g * P:(g + 1) * P], m01[:, g * P:(g + 1) * P],
                sc(QSC + g), sgn[:, g * P:(g + 1) * P],
                ALU.mult, ALU.mult),
                reads=[("rX_m", g), ("rX_sgn",), ("c_qsc",)] +
                      rXb(1536, 1536 + W) + rXb(2432, 2432 + W),
                writes=[("rX_qbf", pp, g)] + (rqbf if g == 0 else []))
        OPS("sync", lambda e: e.dma_start(out=dst_ap, in_=qbf),
            reads=[("rX_qbf", pp, g) for g in range(G)] + rqbf,
            writes=[("wag_w",)], is_dma=True)

    quant_jobs = []
    for r in range(NF):
        quant_jobs.append((wg_sh[r * P:(r + 1) * P, :],
                           wag_g_in[r * P:(r + 1) * P, :], 2))
    for r in range(NF):
        quant_jobs.append((wu_sh[r * P:(r + 1) * P, :],
                           wag_g_in[DFF + r * P:DFF + (r + 1) * P, :], 2))
    for r in range(NI):
        quant_jobs.append((wd_sh[r * P:(r + 1) * P, :],
                           wag_d_in[r * P:(r + 1) * P, :], WD_PAD))
    quant_jobs.reverse()

    def emit_quant(n):
        for _ in range(n):
            if quant_jobs:
                quant_tile(*quant_jobs.pop())

    def emit_ag1():
        emit_quant(len(quant_jobs))
        if DEBUG:
            OPS("gpsimd", lambda e: e.dma_start(out=dbg["d_wgq"], in_=wag_g_in),
                reads=[("wag_w",)], writes=[("dbgwg",)], is_dma=True)
        OPS("gpsimd", lambda e: e.collective_compute(
            "AllGather", ALU.bypass, replica_groups=[list(range(RNK))],
            ins=[wag_g_in], outs=[wag_g_out]),
            reads=[("wag_w",), ("qf", 15, 1)], writes=[("wag_out",)],
            stream=("cc", "1a"))
        OPS("gpsimd", lambda e: e.collective_compute(
            "AllGather", ALU.bypass, replica_groups=[list(range(RNK))],
            ins=[wag_d_in], outs=[wag_d_out]),
            reads=[("wag_w",), ("qf", 15, 1)], writes=[("wag_outd",)],
            stream=("cc", "1b"))

    def load_wslice(wdram, ic, oq, par, group=None):
        OPS("gpsimd", lambda e, ic=ic, oq=oq, par=par, wdram=wdram: e.dma_start(
            out=wbuf[par], in_=wdram[ic * P:(ic + 1) * P,
                                     oq * 512:(oq + 1) * 512]),
            writes=[("wbuf", par)], is_dma=True, group=group)

    for oq in range(4):
        for blk in range(0, NI, 4):
            for ic in range(blk, blk + 4):
                load_wslice(wv_d, ic, oq, ic % 4, group=("vw", oq, blk // 8))
            for ic in range(blk, blk + 4):
                par = ic % 4
                first = ic == 0
                for tt in range(NT):
                    OPS("tensor", lambda e, ic=ic, tt=tt, par=par: e.matmul(
                        bank(tt), x_fm[ic][:, tt * P:(tt + 1) * P], wbuf[par],
                        start=(ic == 0), stop=False),
                        reads=[rA(ic), ("wbuf", par)],
                        acquire=psB(tt) if (first and tt >= 0) else ())
        for tt in range(NT):
            OPS("tensor", lambda e, oq=oq, tt=tt: e.matmul(
                bank(tt), ones_row[:], brow[:, oq * 512:(oq + 1) * 512],
                start=False, stop=True),
                reads=[("ones_r",), ("brow",)], commit=psB(tt))
        for tt in range(NT):
            OPS("scalar", lambda e, oq=oq, tt=tt: e.copy(
                v_tm[tt][:, oq * 512:(oq + 1) * 512], bank(tt)),
                reads=psB(tt), writes=[rD(tt), ("vtm", tt, oq)])
        emit_quant(15)

    def proj_fm(wdram, pb, nb, pres, nres, dst_fm, dblk, dres, qn=0,
                hook=None):
        for rnd in range(4):
            for blk in range(0, NI, 4):
                for ic in range(blk, blk + 4):
                    load_wslice(wdram, ic, rnd, ic % 4, group=(id(wdram), rnd, blk // 8))
                for ic in range(blk, blk + 4):
                    par = ic % 4
                    for j in range(4):
                        for half in range(2):
                            b = 2 * j + half
                            first = ic == 0; last = ic == NI - 1
                            OPS("tensor",
                                lambda e, ic=ic, j=j, half=half, par=par, b=b: e.matmul(
                                    bank(b), wbuf[par][:, j * P:(j + 1) * P],
                                    x_fm[ic][:, half * 512:(half + 1) * 512],
                                    start=(ic == 0), stop=(ic == NI - 1)),
                                reads=[rA(ic), ("wbuf", par)],
                                acquire=psB(b) if first else (),
                                commit=psB(b) if last else ())
            if rnd == 0 and hook is not None:
                hook()
            for j in range(4):
                oc = rnd * 4 + j
                for half in range(2):
                    b = 2 * j + half
                    OPS("scalar", lambda e, b=b, oc=oc: e.activation(
                        t1[:], bank(b), AF.Relu, bias=sc(nb + oc), scale=-1.0),
                        reads=psB(b) + [nres], writes=[("t1",)])
                    OPS("scalar", lambda e: e.activation(
                        t2[:], t1[:], AF.Exp, scale=-1.0),
                        reads=[("t1",)], writes=[("t2",)])
                    OPS("scalar", lambda e, b=b, oc=oc: e.activation(
                        t3[:], bank(b), AF.Relu, bias=sc(pb + oc), scale=1.0),
                        reads=psB(b) + [pres], writes=[("t3",)])
                    OPS("vector",
                        lambda e, oc=oc, half=half: e.tensor_add(
                            dst_fm[oc][:, half * 512:(half + 1) * 512],
                            t2[:], t3[:]),
                        reads=[("t2",), ("t3",)],
                        writes=[dblk(oc), (dres, oc, half)])
            emit_quant(qn[rnd] if isinstance(qn, (list, tuple)) else qn)

    proj_fm(wk_d, BK, NBK, ("c_bk",), ("c_nbk",), kf_fm, rCf, "kf", qn=[14, 14, 9, 7])

    for oc in range(NI):
        for tt in range(NT):
            s8 = (oc * NT + tt) % 8
            pcol = s8 * 1024
            pres = psq(s8, 0)
            OPS("tensor", lambda e, oc=oc, tt=tt, pcol=pcol: e.transpose(
                PS_bf[:, pcol:pcol + P],
                kf_fm[oc][:, tt * P:(tt + 1) * P], ident_bf[:]),
                reads=[("kf", oc, tt // 4), ("ident",)],
                writes=[pres])
            OPS("scalar", lambda e, oc=oc, tt=tt, pcol=pcol: e.copy(
                kf_tm[tt][:, oc * P:(oc + 1) * P],
                PS_bf[:, pcol:pcol + P]),
                reads=[pres], writes=[rCt(tt), ("kt", tt, oc)])

    for h in range(H):
        b = h // 4
        col = (h % 4) * P
        for n in range(NT):
            first = n == 0 and h % 4 == 0
            last = n == NT - 1 and h % 4 == 3
            OPS("tensor", lambda e, h=h, n=n, b=b, col=col: e.matmul(
                PS[:, b * 512 + col:b * 512 + col + P],
                kf_tm[n][:, h * P:(h + 1) * P], v_tm[n][:, h * P:(h + 1) * P],
                start=(n == 0), stop=(n == NT - 1)),
                reads=[("kt", n, h), ("vtm", n, h // 4), rCt(n), rD(n)],
                acquire=[psq(b, h % 4)] if n == 0 else (),
                commit=[psq(b, h % 4)] if n == NT - 1 else ())
    for h in range(H):
        for n in range(NT):
            first = n == 0 and h == 0
            last = n == NT - 1 and h == H - 1
            OPS("tensor", lambda e, h=h, n=n: e.matmul(
                PS[:, 4 * 512 + h:4 * 512 + h + 1],
                kf_tm[n][:, h * P:(h + 1) * P], ones_col[:],
                start=(n == 0), stop=(n == NT - 1)),
                reads=[("kt", n, h), ("ones_c",), rCt(n)],
                acquire=[psq(4, 0)] if first else (),
                commit=[psq(4, 0)] if last else ())
    for b4 in range(4):
        tb = t1 if b4 % 2 == 0 else t2
        rt = ("t1",) if b4 % 2 == 0 else ("t2",)
        OPS("vector", lambda e, b4=b4, tb=tb: e.tensor_copy(tb[:], bank(b4)),
            reads=psB(b4), writes=[rt])
        OPS("sync", lambda e, b4=b4, tb=tb: e.dma_start(
            out=kag_in[:, b4 * 512:(b4 + 1) * 512], in_=tb[:]),
            reads=[rt], writes=[("kag_w",)], is_dma=True)
    OPS("vector", lambda e: e.tensor_copy(t3[:, 0:H], PS[:, 4 * 512:4 * 512 + H]),
        reads=[psq(4, 0)], writes=[("t3",)])
    OPS("sync", lambda e: e.dma_start(out=kag_in[:, DM:DM + H], in_=t3[:, 0:H]),
        reads=[("t3",)], writes=[("kag_w",)], is_dma=True)
    if DEBUG:
        OPS("sync", lambda e: e.dma_start(out=dbg["d_kvtot"], in_=kag_in),
            reads=[("kag_w",)], writes=[("dbg_kv",)], is_dma=True)
    def ag2_prefix_hook():
        OPS("gpsimd", lambda e: e.collective_compute(
            "AllGather", ALU.bypass, replica_groups=[list(range(RNK))],
            ins=[kag_in], outs=[kag_out]),
            reads=[("kag_w",)], writes=[("kag_out",)], stream=("cc", "2"))
        for r in range(RNK):
            if r % 2 == 0:
                st_kv = regX[:, 0:2048]
                r_kv = rXb(0, 2048)
                st_ks = regX[:, 2048:2048 + H]
                r_ks = rXb(2048, 2048 + H)
            else:
                st_kv = regB_f32q[:, 6144:8192]
                r_kv = [("rB", bb) for bb in range(12, 16)]
                st_ks = regX[:, 2064:2064 + H]
                r_ks = rXb(2064, 2064 + H)
            OPS("scalar", lambda e, r=r, st_kv=st_kv: e.dma_start(
                out=st_kv, in_=kag_out[r * P:(r + 1) * P, 0:DM]),
                reads=[("kag_out",)], writes=r_kv, is_dma=True)
            OPS("scalar", lambda e, r=r, st_ks=st_ks: e.dma_start(
                out=st_ks, in_=kag_out[r * P:(r + 1) * P, DM:DM + H]),
                reads=[("kag_out",)], writes=r_ks, is_dma=True)
            OPS("vector", lambda e, r=r, st_kv=st_kv: e.scalar_tensor_tensor(
                Skv[:], st_kv, pm_sb[:, r:r + 1], Skv[:], ALU.mult, ALU.add),
                reads=r_kv + [("pm",), ("Skv",)], writes=[("Skv",)])
            OPS("vector", lambda e, r=r, st_ks=st_ks: e.scalar_tensor_tensor(
                Sks[:], st_ks, pm_sb[:, r:r + 1], Sks[:], ALU.mult, ALU.add),
                reads=r_ks + [("pm",), ("Sks",)], writes=[("Sks",)])
        OPS("vector", lambda e: e.tensor_copy(Skv_bf[:], Skv[:]),
            reads=[("Skv",)], writes=[("Skvb",)])
        OPS("vector", lambda e: e.tensor_copy(Sks_bf[:], Sks[:]),
            reads=[("Sks",)], writes=[("Sksb",)])

    proj_fm(wq_d, BQ, NBQ, ("c_bq",), ("c_nbq",), qf_fm, rB, "qf",
            hook=ag2_prefix_hook)

    if DEBUG:
        for ic in range(NI):
            OPS("gpsimd", lambda e, ic=ic: e.dma_start(
                out=dbg["d_qf"][ic * P:(ic + 1) * P, :], in_=qf_fm[ic]),
                reads=[("qf", ic, 0), ("qf", ic, 1), rB(ic)], writes=[("dbgq",)],
                is_dma=True)
            OPS("gpsimd", lambda e, ic=ic: e.dma_start(
                out=dbg["d_kf"][ic * P:(ic + 1) * P, :], in_=kf_fm[ic]),
                reads=[("kf", ic, 0), ("kf", ic, 1), rCf(ic)], writes=[("dbgk",)],
                is_dma=True)
        for tt in range(NT):
            OPS("gpsimd", lambda e, tt=tt: e.dma_start(
                out=dbg["d_v"][tt * P:(tt + 1) * P, :], in_=v_tm[tt]),
                reads=[("vtm", tt, q) for q in range(4)] + [rD(tt)], writes=[("dbgv",)],
                is_dma=True)

    emit_ag1()

    for n in range(NT):
        for h in range(H):
            b = h // 4
            col = (h % 4) * P
            OPS("tensor", lambda e, h=h, n=n, b=b, col=col: e.matmul(
                PS[:, b * 512 + col:b * 512 + col + P],
                kf_tm[n][:, h * P:(h + 1) * P], v_tm[n][:, h * P:(h + 1) * P],
                start=True, stop=True),
                reads=[("kt", n, h), ("vtm", n, h // 4), rCt(n), rD(n)],
                writes=[psq(b, h % 4)])
        for h in range(H):
            OPS("tensor", lambda e, h=h, n=n: e.matmul(
                PS[:, 4 * 512 + h:4 * 512 + h + 1],
                kf_tm[n][:, h * P:(h + 1) * P], ones_col[:],
                start=True, stop=True),
                reads=[("kt", n, h), ("ones_c",), rCt(n)],
                acquire=[psq(4, 0)] if h == 0 else (),
                commit=[psq(4, 0)] if h == H - 1 else ())
        def pe_A(h):
            par = h % 2
            OPS("tensor", lambda e, h=h, n=n, par=par: e.matmul(
                PS[:, 5 * 512 + par * P:5 * 512 + (par + 1) * P],
                kf_fm[h][:, n * P:(n + 1) * P], qf_fm[h][:, n * P:(n + 1) * P],
                start=True, stop=True),
                reads=[("kf", h, n // 4), ("qf", h, n // 4), rCf(h), rB(h)],
                writes=[psq(5, par)])

        def dve_mask(h):
            par = h % 2
            OPS("vector", lambda e, par=par: e.tensor_mul(
                Asb[par], PS[:, 5 * 512 + par * P:5 * 512 + (par + 1) * P],
                mask_f[:]),
                reads=[psq(5, par), ("mask",)], writes=[("Asb", par)])

        def pe_I(h):
            par = h % 2
            OPS("tensor", lambda e, h=h, n=n, par=par: e.matmul(
                PS[:, 6 * 512 + par * P:6 * 512 + (par + 1) * P],
                Asb[par], v_tm[n][:, h * P:(h + 1) * P],
                start=True, stop=False),
                reads=[("Asb", par), ("vtm", n, h // 4), rD(n)],
                acquire=[psq(6, par)])
            OPS("tensor", lambda e, h=h, n=n, par=par: e.matmul(
                PS[:, 6 * 512 + par * P:6 * 512 + (par + 1) * P],
                qf_fm[h][:, n * P:(n + 1) * P], Skv_bf[:, h * P:(h + 1) * P],
                start=False, stop=True),
                reads=[("qf", h, n // 4), ("Skvb",), rB(h)],
                commit=[psq(6, par)])
            OPS("tensor", lambda e, h=h, n=n: e.matmul(
                PS[:, 4 * 512 + 16 + h:4 * 512 + 17 + h],
                Asb[h % 2], ones_col[:], start=True, stop=False),
                reads=[("Asb", h % 2), ("ones_c",)], acquire=[("ps4z", h)])
            OPS("tensor", lambda e, h=h, n=n: e.matmul(
                PS[:, 4 * 512 + 16 + h:4 * 512 + 17 + h],
                qf_fm[h][:, n * P:(n + 1) * P], Sks_bf[:, h:h + 1],
                start=False, stop=True),
                reads=[("qf", h, n // 4), ("Sksb",), rB(h)],
                commit=[("ps4z", h)])

        def act_zeps(h):
            OPS("scalar", lambda e, h=h: e.activation(
                sc(ZEPS + h), PS[:, 4 * 512 + 16 + h:4 * 512 + 17 + h],
                AF.Identity, bias=scl[:, 123:124]),
                reads=[("ps4z", h), ("c_eps",)], writes=[("zeps", h)])

        def dve_div(h):
            par = h % 2
            OPS("vector", lambda e, h=h: e.reciprocal(sc(ZR + h), sc(ZEPS + h)),
                reads=[("zeps", h)], writes=[("zr", h)])
            OPS("vector", lambda e, h=h, par=par: e.tensor_scalar(
                attm[par], PS[:, 6 * 512 + par * P:6 * 512 + (par + 1) * P],
                sc(ZR + h), None, ALU.mult),
                reads=[psq(6, par), ("zr", h)], writes=[("attm", par)])

        def pe_T(h):
            par = h % 2
            OPS("tensor", lambda e, par=par: e.transpose(
                PS_bf[:, 7 * 1024 + par * 256:7 * 1024 + par * 256 + P],
                attm[par], ident_bf[:]),
                reads=[("attm", par), ("ident",)], writes=[psq(7, par)])

        def act_evacA(h):
            par = h % 2
            OPS("scalar", lambda e, h=h, n=n, par=par: e.copy(
                att_fm[h][:, n * P:(n + 1) * P],
                PS_bf[:, 7 * 1024 + par * 256:7 * 1024 + par * 256 + P]),
                reads=[psq(7, par)], writes=[rA(h), ("af", h, n)])

        pe_A(0); dve_mask(0)
        pe_A(1); dve_mask(1)
        for h in range(H):
            pe_I(h)
            act_zeps(h)
            if h + 2 < H:
                pe_A(h + 2)
                dve_mask(h + 2)
            dve_div(h)
            pe_T(h)
            act_evacA(h)
        for b4 in range(4):
            OPS("vector", lambda e, b4=b4: e.tensor_add(
                Skv[:, b4 * 512:(b4 + 1) * 512],
                Skv[:, b4 * 512:(b4 + 1) * 512], bank(b4)),
                reads=psB(b4) + [("Skv",)], writes=[("Skv",)])
        OPS("vector", lambda e: e.tensor_add(
            Sks[:], Sks[:], PS[:, 4 * 512:4 * 512 + H]),
            reads=[psq(4, 0), ("Sks",)], writes=[("Sks",)])
        OPS("vector", lambda e: e.tensor_copy(Skv_bf[:], Skv[:]),
            reads=[("Skv",)], writes=[("Skvb",)])
        OPS("vector", lambda e: e.tensor_copy(Sks_bf[:], Sks[:]),
            reads=[("Sks",)], writes=[("Sksb",)])

    if DEBUG:
        for ic in range(NI):
            OPS("gpsimd", lambda e, ic=ic: e.dma_start(
                out=dbg["d_att"][ic * P:(ic + 1) * P, :], in_=att_fm[ic]),
                reads=[("af", ic, n2) for n2 in range(NT)] + [rA(ic)],
                writes=[("dbga",)], is_dma=True)

    OPS("gpsimd", lambda e: e.dma_start(
        out=brow[:], in_=bo_d.unsqueeze(0)),
        writes=[("brow",)], is_dma=True)
    xs_stage = [regX[:, 0:512], regX[:, 512:1024]]
    for oq in range(4):
        for blk in range(0, NI, 4):
            for ic in range(blk, blk + 4):
                load_wslice(wo_d, ic, oq, ic % 4, group=("wow", oq, blk // 8))
            for ic in range(blk, blk + 4):
                par = ic % 4
                first = ic == 0
                for tt in range(NT):
                    OPS("tensor", lambda e, ic=ic, tt=tt, par=par: e.matmul(
                        bank(tt), att_fm[ic][:, tt * P:(tt + 1) * P], wbuf[par],
                        start=(ic == 0), stop=False),
                        reads=[("af", ic, tt), ("wbuf", par), rA(ic)],
                        acquire=psB(tt) if first else ())
        for tt in range(NT):
            OPS("tensor", lambda e, oq=oq, tt=tt: e.matmul(
                bank(tt), ones_row[:], brow[:, oq * 512:(oq + 1) * 512],
                start=False, stop=True),
                reads=[("ones_r",), ("brow",)], commit=psB(tt))
        for tt in range(NT):
            par = tt % 2
            OPS("sync", lambda e, oq=oq, tt=tt, par=par: e.dma_start(
                out=xs_stage[par],
                in_=x_sl[tt * P:(tt + 1) * P, oq * 512:(oq + 1) * 512]),
                writes=rXb(par * 512, par * 512 + 512), is_dma=True)
            OPS("vector", lambda e, oq=oq, tt=tt, par=par: e.tensor_add(
                x1n[tt][:, oq * 512:(oq + 1) * 512], xs_stage[par], bank(tt)),
                reads=rXb(par * 512, par * 512 + 512) + psB(tt),
                writes=[rC_x1(tt, oq), ("x1", tt, oq)])

    for tt in range(NT):
        rx1 = [("x1", tt, q) for q in range(4)]
        for q4 in range(4):
            OPS("vector", lambda e, tt=tt, q4=q4: e.bn_stats(
                bnst[:, q4, :], x1n[tt][:, q4 * 512:(q4 + 1) * 512]),
                reads=[("x1", tt, q4)], writes=[("bnst", q4)])
        OPS("vector", lambda e: e.bn_aggr(
            sc(BNA, 2), bnst.rearrange("p a b -> p (a b)")),
            reads=[("bnst", q) for q in range(4)], writes=[("c_bna",)])
        OPS("vector", lambda e: e.tensor_scalar(
            sc(LNV), sc(BNA + 1), EPS_LN, None, ALU.add),
            reads=[("c_bna",)], writes=[("c_lnv",)])
        OPS("scalar", lambda e: e.activation(sc(LNV), sc(LNV), AF.Ln),
            reads=[("c_lnv",)], writes=[("c_lnv",)])
        OPS("scalar", lambda e: e.activation(
            sc(LNR), sc(LNV), AF.Exp, scale=-0.5),
            reads=[("c_lnv",)], writes=[("c_lnr",)])
        OPS("vector", lambda e: e.scalar_tensor_tensor(
            sc(LNB), sc(BNA), -1.0, sc(LNR), ALU.mult, ALU.mult),
            reads=[("c_bna",), ("c_lnr",)], writes=[("c_lnb",)])
        OPS("scalar", lambda e, tt=tt: e.activation(
            x1n[tt], x1n[tt], AF.Identity, bias=sc(LNB), scale=sc(LNR)),
            reads=rx1 + [("c_lnb",), ("c_lnr",)],
            writes=[("x1", tt, q) for q in range(4)] +
                   [rC_x1(tt, q) for q in range(4)])
        OPS("vector", lambda e, tt=tt: e.tensor_reduce(
            sc(AMX), x1n[tt], AX, ALU.max, apply_absolute_value=True),
            reads=rx1, writes=[("c_amx",)])
        OPS("vector", lambda e: e.tensor_scalar(
            sc(AMX), sc(AMX), EPS_FFN, None, ALU.max),
            reads=[("c_amx",)], writes=[("c_amx",)])
        OPS("vector", lambda e: e.reciprocal(sc(SFC), sc(AMX)),
            reads=[("c_amx",)], writes=[("c_sfc",)])
        OPS("vector", lambda e: e.tensor_scalar(
            sc(SFC), sc(SFC), 127.0, None, ALU.mult),
            reads=[("c_sfc",)], writes=[("c_sfc",)])
        OPS("vector", lambda e: e.reciprocal(sc(ISF), sc(SFC)),
            reads=[("c_sfc",)], writes=[("c_isf",)])
        for hh in range(2):
            sl = slice(hh * 1024, (hh + 1) * 1024)
            ti = regX_i32[:, 0:1024]
            e_ = regX[:, 1024:2048]
            f_ = regX[:, 2048:3072]
            OPS("vector", lambda e, tt=tt, sl=sl, ti=ti: e.tensor_scalar(
                ti, x1n[tt][:, sl], sc(SFC), None, ALU.mult),
                reads=rx1 + [("c_sfc",)], writes=rXb(0, 1024))
            OPS("vector", lambda e, tt=tt, sl=sl, ti=ti, e_=e_:
                e.scalar_tensor_tensor(
                    e_, x1n[tt][:, sl], sc(SFC), ti, ALU.mult, ALU.subtract),
                reads=rx1 + [("c_sfc",)] + rXb(0, 1024),
                writes=rXb(1024, 2048))
            OPS("vector", lambda e, e_=e_, f_=f_: e.tensor_scalar(
                f_, e_, -0.5, None, ALU.is_lt),
                reads=rXb(1024, 2048), writes=rXb(2048, 3072))
            OPS("vector", lambda e, ti=ti, e_=e_: e.scalar_tensor_tensor(
                e_, e_, 0.5, ti, ALU.is_gt, ALU.add),
                reads=rXb(1024, 2048) + rXb(0, 1024), writes=rXb(1024, 2048))
            OPS("vector", lambda e, e_=e_, f_=f_: e.tensor_sub(
                e_, e_, f_),
                reads=rXb(1024, 2048) + rXb(2048, 3072),
                writes=rXb(1024, 2048))
            OPS("scalar", lambda e, tt=tt, sl=sl, e_=e_: e.mul(
                xq_tm[tt][:, sl], e_, sc(ISF)),
                reads=rXb(1024, 2048) + [("c_isf",)],
                writes=[rD(tt), ("xqt", tt, sl.start)])
        OPS("sync", lambda e, tt=tt: e.dma_start(
            out=xq_dr[tt * P:(tt + 1) * P, :], in_=xq_tm[tt]),
            reads=[("xqt", tt, 0), ("xqt", tt, 1024), rD(tt)],
            writes=[("xqd", tt)], is_dma=True)
        if DEBUG:
            OPS("sync", lambda e, tt=tt: e.dma_start(
                out=dbg["d_x1n"][tt * P:(tt + 1) * P, :], in_=x1n[tt]),
                reads=[("x1", tt, q) for q in range(4)] +
                      [rC_x1(tt, q) for q in range(4)],
                writes=[("dbgx1",)], is_dma=True)
            OPS("gpsimd", lambda e, tt=tt: e.dma_start(
                out=dbg["d_xq"][tt * P:(tt + 1) * P, :], in_=xq_tm[tt]),
                reads=[("xqt", tt, 0), ("xqt", tt, 1024), rD(tt)], writes=[("dbgxq",)], is_dma=True)

    for ic in range(NI):
        OPS("sync", lambda e, ic=ic: e.dma_start_transpose(
            xq_fm[ic], xq_dr[:, ic * P:(ic + 1) * P]),
            reads=[("xqd", tt2) for tt2 in range(NT)],
            writes=[rB(ic), ("xqf", ic)], is_dma=True, group=("xqfm",))

    for os in range(len(OS_W)):
        w = OS_W[os]; o0 = OS_OFF[os]
        par = os % 2
        gres, ures = [], []
        for ic in range(NI):
            r = ic // 2
            res_g = ("f1wg", par, ic)
            OPS("sync", lambda e, ic=ic, r=r, o0=o0, w=w, par=par:
                e.dma_start_transpose(
                    wgbuf(par, ic)[:, 0:w],
                    wag_g_out[r * GU_ROWS + o0:r * GU_ROWS + o0 + w,
                              (ic % 2) * P:(ic % 2 + 1) * P]),
                reads=[("wag_out",)], writes=[rWg(par, ic), res_g],
                is_dma=True, group=("f1w", os))
            gres.append(res_g)
            res_u = ("f1wu", par, ic)
            OPS("sync", lambda e, ic=ic, r=r, o0=o0, w=w, par=par:
                e.dma_start_transpose(
                    wubuf(par, ic)[:, 0:w],
                    wag_g_out[r * GU_ROWS + DFF + o0:r * GU_ROWS + DFF + o0 + w,
                              (ic % 2) * P:(ic % 2 + 1) * P]),
                reads=[("wag_out",)], writes=[rWu(par, ic), res_u],
                is_dma=True, group=("f1w", os))
            ures.append(res_u)
        for half in range(2):
            tts = range(half * 4, half * 4 + 4)
            for ic in range(NI):
                for tt in tts:
                    bg = tt % 4; bu = 4 + tt % 4
                    first = ic == 0; last = ic == NI - 1
                    OPS("tensor",
                        lambda e, ic=ic, tt=tt, par=par, bg=bg, w=w: e.matmul(
                            bank(bg, w), xq_fm[ic][:, tt * P:(tt + 1) * P],
                            wgbuf(par, ic)[:, 0:w],
                            start=(ic == 0), stop=(ic == NI - 1)),
                        reads=[("xqf", ic), gres[ic], rWg(par, ic), rB(ic)],
                        acquire=psB(bg, w) if first else (),
                        commit=psB(bg, w) if last else ())
                    OPS("tensor",
                        lambda e, ic=ic, tt=tt, par=par, bu=bu, w=w: e.matmul(
                            bank(bu, w), xq_fm[ic][:, tt * P:(tt + 1) * P],
                            wubuf(par, ic)[:, 0:w],
                            start=(ic == 0), stop=(ic == NI - 1)),
                        reads=[("xqf", ic), ures[ic], rWu(par, ic), rB(ic)],
                        acquire=psB(bu, w) if first else (),
                        commit=psB(bu, w) if last else ())
            for tt in tts:
                bg = tt % 4; bu = 4 + tt % 4
                te, rte = (t1, ("t1",)) if tt % 2 == 0 else (t2, ("t2",))
                to, rto = (t3, ("t3",)) if tt % 2 == 0 else (t1, ("t1b",))
                if tt % 2 == 1:
                    rto = ("t1",)
                uc0 = 1376 + (tt % 2) * 512
                ucp = regX[:, uc0:uc0 + 512]
                rucp = [("f1u", tt % 2)] + rXb(uc0, uc0 + 512)
                OPS("scalar", lambda e, bg=bg, w=w, te=te: e.activation(
                    te[:, 0:w], bank(bg, w), AF.Silu),
                    reads=psB(bg, w), writes=[rte])
                OPS("scalar", lambda e, bu=bu, w=w, ucp=ucp: e.copy(
                    ucp[:, 0:w], bank(bu, w)),
                    reads=psB(bu, w), writes=rucp)
                OPS("vector", lambda e, w=w, te=te, to=to, ucp=ucp:
                    e.tensor_mul(to[:, 0:w], te[:, 0:w], ucp[:, 0:w]),
                    reads=[rte] + rucp, writes=[rto])
                OPS("vector", lambda e, w=w, to=to: e.tensor_reduce(
                    sc(AMXT), to[:, 0:w], AX, ALU.max,
                    apply_absolute_value=True),
                    reads=[rto], writes=[("c_amxt",)])
                OPS("vector", lambda e, tt=tt: e.tensor_max(
                    hamax[:, tt:tt + 1], hamax[:, tt:tt + 1], sc(AMXT)),
                    reads=[("c_amxt",), ("ham", tt)], writes=[("ham", tt)])
                OPS("scalar", lambda e, tt=tt, o0=o0, w=w, to=to: e.dma_start(
                    out=hid_dr[tt * P:(tt + 1) * P, o0:o0 + w], in_=to[:, 0:w]),
                    reads=[rto], writes=[("hidd", tt, os)], is_dma=True)

    regD_f32 = regD.bitcast(F32)
    f2_e_ = regD_f32[:, 3072:4448]
    r_f2e = [("rD", 3), ("rD", 4)]
    f2_hidq = [
        (regD_f32[:, 4448:5824], [("rD", 4), ("rD", 5)]),
        (regD_f32[:, 5824:7200], [("rD", 5), ("rD", 6), ("rD", 7)]),
        (regB_f32[:, 5120:6496], [("rB", 10), ("rB", 11), ("rB", 12)]),
    ]
    f2_hqq = [
        (regB[:, 12992:14368], [("rB", 12), ("rB", 13)]),
        (regB[:, 14368:15744], [("rB", 14), ("rB", 15)]),
    ]
    f2_ti = regX_i32[:, 0:1376]
    f2_f_ = regX[:, 1376:2752]
    HAM2S = [HAM2, 124]
    SF2S = [SF2, 125]
    IS2S = [IS2, 126]
    _f2j = [0]

    def f2_job(tt, q):
        j = _f2j[0]; _f2j[0] += 1
        hidq, rh = f2_hidq[j % 3]
        hqq, rq = f2_hqq[j % 2]
        sH, sS, sI = HAM2S[j % 2], SF2S[j % 2], IS2S[j % 2]
        q0 = q * 1376
        os_lo = q0 // 512
        os_hi = (q0 + 1375) // 512
        OPS("gpsimd", lambda e, tt=tt, q0=q0, hidq=hidq: e.dma_start(
            out=hidq, in_=hid_dr[tt * P:(tt + 1) * P, q0:q0 + 1376]),
            reads=[("hidd", tt, os2) for os2 in range(os_lo, os_hi + 1)],
            writes=rh, is_dma=True)
        OPS("vector", lambda e, tt=tt, sH=sH: e.tensor_scalar(
            sc(sH), hamax[:, tt:tt + 1], EPS_FFN, None, ALU.max),
            reads=[("ham", tt)], writes=[("c_ham2", sH)])
        OPS("vector", lambda e, sH=sH, sS=sS: e.reciprocal(sc(sS), sc(sH)),
            reads=[("c_ham2", sH)], writes=[("c_sf2", sS)])
        OPS("vector", lambda e, sS=sS: e.tensor_scalar(
            sc(sS), sc(sS), 127.0, None, ALU.mult),
            reads=[("c_sf2", sS)], writes=[("c_sf2", sS)])
        OPS("vector", lambda e, sS=sS, sI=sI: e.reciprocal(sc(sI), sc(sS)),
            reads=[("c_sf2", sS)], writes=[("c_is2", sI)])
        OPS("vector", lambda e, hidq=hidq, sS=sS: e.tensor_scalar(
            f2_ti, hidq, sc(sS), None, ALU.mult),
            reads=rh + [("c_sf2", sS)], writes=rXb(0, 1376))
        OPS("vector", lambda e, hidq=hidq, sS=sS: e.scalar_tensor_tensor(
            f2_e_, hidq, sc(sS), f2_ti, ALU.mult, ALU.subtract),
            reads=rh + rXb(0, 1376) + [("c_sf2", sS)], writes=r_f2e)
        OPS("vector", lambda e: e.tensor_scalar(
            f2_f_, f2_e_, -0.5, None, ALU.is_lt),
            reads=r_f2e, writes=rXb(1376, 2752))
        OPS("vector", lambda e: e.scalar_tensor_tensor(
            f2_e_, f2_e_, 0.5, f2_ti, ALU.is_gt, ALU.add),
            reads=r_f2e + rXb(0, 1376), writes=r_f2e)
        OPS("vector", lambda e: e.tensor_sub(f2_e_, f2_e_, f2_f_),
            reads=r_f2e + rXb(1376, 2752), writes=r_f2e)
        OPS("scalar", lambda e, hqq=hqq, sI=sI: e.mul(hqq, f2_e_, sc(sI)),
            reads=r_f2e + [("c_is2", sI)], writes=rq + [("f2hq", j % 2)])
        OPS("scalar", lambda e, tt=tt, q0=q0, hqq=hqq, j=j: e.dma_start(
            out=hq_dr[tt * P:(tt + 1) * P, q0:q0 + 1376], in_=hqq),
            reads=rq + [("f2hq", j % 2)],
            writes=[("hqd", tt), ("hqd2", tt, q)], is_dma=True)

    def emit_hqfm(half, qlevel):
        hrow0 = half * 512
        tts = list(range(half * 4, half * 4 + 4))
        for fc in range(NF):
            if (fc * P + P - 1) // 1376 != qlevel:
                continue
            OPS("sync", lambda e, fc=fc, hrow0=hrow0: e.dma_start_transpose(
                hq_fm(fc), hq_dr[hrow0:hrow0 + 512, fc * P:(fc + 1) * P]),
                reads=[("hqd2", tt2, qq) for tt2 in tts
                       for qq in range(qlevel + 1)],
                writes=[rHq(fc), ("hqf", fc)], is_dma=True,
                group=("hqfm", half, qlevel))

    def f3_half(half, post_pair=None):
        tts = list(range(half * 4, half * 4 + 4))
        emitted_lvl = [-1]

        def need_lvl(lvl):
            while emitted_lvl[0] < lvl:
                emitted_lvl[0] += 1
                emit_hqfm(half, emitted_lvl[0])

        for pair in range(2):
            for gidx, fb2 in enumerate(range(0, NF, 2)):
                fcs = [f for f in (fb2, fb2 + 1) if f < NF]
                if pair == 0:
                    look = min(fb2 + 3, NF - 1)
                    need_lvl((look * P + P - 1) // 1376)
                for fc in fcs:
                    for oq2 in range(2):
                        oq4 = pair * 2 + oq2
                        par = 2 * (fc - fb2) + oq2
                        rr = next(r for r in range(RNK)
                                  if WD_RANGES[r][0] <= fc < WD_RANGES[r][1])
                        lcol = (fc - WD_RANGES[rr][0]) * P
                        OPS("sync", lambda e, oq4=oq4, par=par, rr=rr,
                            lcol=lcol: e.dma_start_transpose(
                                wdbuf[par],
                                wag_d_out[rr * DM + oq4 * 512:
                                          rr * DM + (oq4 + 1) * 512,
                                          lcol:lcol + P]),
                            reads=[("wag_outd",)],
                            writes=[("wdb", par), ("rB", 8 + par // 2)],
                            is_dma=True, group=("wdw", half, pair, fb2))
                for fc in fcs:
                    for oq2 in range(2):
                        par = 2 * (fc - fb2) + oq2
                        for tt in tts:
                            b = (tt % 4) * 2 + oq2
                            OPS("tensor",
                                lambda e, fc=fc, tt=tt, par=par, b=b: e.matmul(
                                    bank(b),
                                    hq_fm(fc)[:, (tt % 4) * P:(tt % 4 + 1) * P],
                                    wdbuf[par],
                                    start=(fc == 0), stop=(fc == NF - 1)),
                                reads=[("hqf", fc), ("wdb", par),
                                       ("rB", 8 + par // 2), rHq(fc)],
                                acquire=psB(b) if fc == 0 else (),
                                commit=psB(b) if fc == NF - 1 else ())
            for tt in tts:
                for oq2 in range(2):
                    b = (tt % 4) * 2 + oq2
                    oq4 = pair * 2 + oq2
                    OPS("vector", lambda e, tt=tt, oq4=oq4, b=b: e.tensor_add(
                        x1n[tt][:, oq4 * 512:(oq4 + 1) * 512],
                        x1n[tt][:, oq4 * 512:(oq4 + 1) * 512], bank(b)),
                        reads=psB(b) + [("x1", tt, oq4)],
                        writes=[("x1", tt, oq4), rC_x1(tt, oq4)])
            if post_pair is not None:
                post_pair(pair)
        for tt in tts:
            rx2 = [("x1", tt, q) for q in range(4)]
            for q4 in range(4):
                OPS("vector", lambda e, tt=tt, q4=q4: e.bn_stats(
                    bnst[:, q4, :], x1n[tt][:, q4 * 512:(q4 + 1) * 512]),
                    reads=[("x1", tt, q4)], writes=[("bnst", q4)])
            OPS("vector", lambda e: e.bn_aggr(
                sc(BNA, 2), bnst.rearrange("p a b -> p (a b)")),
                reads=[("bnst", q) for q in range(4)], writes=[("c_bna",)])
            OPS("vector", lambda e: e.tensor_scalar(
                sc(LNV), sc(BNA + 1), EPS_LN, None, ALU.add),
                reads=[("c_bna",)], writes=[("c_lnv",)])
            OPS("scalar", lambda e: e.activation(sc(LNV), sc(LNV), AF.Ln),
                reads=[("c_lnv",)], writes=[("c_lnv",)])
            OPS("scalar", lambda e: e.activation(
                sc(LNR), sc(LNV), AF.Exp, scale=-0.5),
                reads=[("c_lnv",)], writes=[("c_lnr",)])
            OPS("vector", lambda e: e.scalar_tensor_tensor(
                sc(LNB), sc(BNA), -1.0, sc(LNR), ALU.mult, ALU.mult),
                reads=[("c_bna",), ("c_lnr",)], writes=[("c_lnb",)])
            par = tt % 2
            ot = out_tile[par]
            OPS("scalar", lambda e, tt=tt, ot=ot: e.activation(
                ot, x1n[tt], AF.Identity, bias=sc(LNB), scale=sc(LNR)),
                reads=rx2 + [("c_lnb",), ("c_lnr",)],
                writes=rOut(par) + [("outt", par)])
            OPS("scalar", lambda e, tt=tt, ot=ot: e.dma_start(
                out=out_sl[tt * P:(tt + 1) * P, :], in_=ot),
                reads=[("outt", par)], writes=[("outd", tt)], is_dma=True)

    if DEBUG:
        for tt in range(NT):
            OPS("gpsimd", lambda e, tt=tt: e.dma_start(
                out=dbg["d_hid"][tt * P:(tt + 1) * P, :],
                in_=hid_dr[tt * P:(tt + 1) * P, :]),
                reads=[("hidd", tt, os2) for os2 in range(len(OS_W))],
                writes=[("dbghid",)], is_dma=True)
    for q in range(4):
        for tt in range(4):
            f2_job(tt, q)

    def post0(pair):
        lvls = (0, 1) if pair == 0 else (2, 3)
        for lvl in lvls:
            for tt in range(4, 8):
                f2_job(tt, lvl)

    f3_half(0, post0)
    if DEBUG:
        for tt in range(NT):
            OPS("gpsimd", lambda e, tt=tt: e.dma_start(
                out=dbg["d_hq"][tt * P:(tt + 1) * P, :],
                in_=hq_dr[tt * P:(tt + 1) * P, :]),
                reads=[("hqd", tt)], writes=[("dbghq",)], is_dma=True)
    f3_half(1)

    OPS("gpsimd", lambda e: e.engine_nop(),
        reads=[("outd", tt) for tt in range(NT)] +
              ([("dbgq",), ("dbgk",), ("dbgv",), ("dbga",), ("dbgx1",),
                ("dbgxq",), ("dbg_kv",), ("dbgwg",), ("dbghid",), ("dbghq",)]
               if DEBUG else []))

    B.finalize(nc, ctx)
    with nc.Block() as block:
        B.replay(block)
    ctx.close()
    return nc


def _contig(a):
    return np.ascontiguousarray(np.asarray(a))


def kernel(L0_attn_x, L0_attn_Wq, L0_attn_bq, L0_attn_Wk, L0_attn_bk,
           L0_attn_Wv, L0_attn_bv, L0_attn_Wo, L0_attn_bo,
           L0_ffn_w_gate, L0_ffn_w_up, L0_ffn_w_down):
    x = np.asarray(L0_attn_x)
    wg = np.asarray(L0_ffn_w_gate); wu = np.asarray(L0_ffn_w_up)
    wd = np.asarray(L0_ffn_w_down)
    nc = build_nc()
    in_maps = []
    for c in range(RNK):
        b, p = c // 4, c % 4
        f0, f1 = WD_RANGES[c]
        wd_pad = np.zeros((DM, WD_PAD * P), np.float32)
        wd_pad[:, 0:(f1 - f0) * P] = wd[:, f0 * P:f1 * P]
        pm = np.zeros((P, RNK), np.float32)
        for r in range(RNK):
            if r // 4 == b and r % 4 < p:
                pm[:, r] = 1.0
        in_maps.append({
            "x_sl": _contig(x[b, p * T:(p + 1) * T]),
            "wq": _contig(L0_attn_Wq), "wk": _contig(L0_attn_Wk),
            "wv": _contig(L0_attn_Wv), "wo": _contig(L0_attn_Wo),
            "bq": _contig(L0_attn_bq), "bk": _contig(L0_attn_bk),
            "bv": _contig(L0_attn_bv), "bo": _contig(L0_attn_bo),
            "wg_sh": _contig(wg[:, 256 * c:256 * (c + 1)]),
            "wu_sh": _contig(wu[:, 256 * c:256 * (c + 1)]),
            "wd_sh": wd_pad,
            "pm": pm,
        })
    res = run_bass_kernel_spmd(nc, in_maps, core_ids=list(range(RNK)))
    out = np.zeros((2, 4096, DM), np.float32)
    for c in range(RNK):
        b, p = c // 4, c % 4
        out[b, p * T:(p + 1) * T] = res.results[c]["out_sl"]
    return out


if __name__ == "__main__":
    nc = build_nc()
    print("build ok, instructions:", nc.n_instructions())
